# version 11
# baseline (speedup 1.0000x reference)
from contextlib import ExitStack

import numpy as np
import ml_dtypes
import concourse.bass as bass
import concourse.mybir as mybir
from concourse.bass_utils import run_bass_kernel_spmd

F32 = mybir.dt.float32
BF16 = mybir.dt.bfloat16
FP8 = mybir.dt.float8e4
U8 = mybir.dt.uint8
ALU = mybir.AluOpType
ACT = mybir.ActivationFunctionType
AXX = mybir.AxisListType.X

B, L, H, I, NST = 4, 2048, 2048, 4096, 16
TOK = 1024
NHT, NIT, NFT, NLT = 16, 32, 48, 8
KB = 1024
SA, CA1, GA, SB_, CB1, GB = 0, 16, 32, 48, 64, 80


def build_program(debug=()):
    nc = bass.Bass("TRN2", target_bir_lowering=False)
    d = {}

    def din(name, shape, dtype=F32):
        d[name] = nc.dram_tensor(name, list(shape), dtype, kind="ExternalInput").ap()

    din("x_own", (TOK, H))
    din("x_pre", (TOK, H))
    din("maskv", (128, 1))
    din("w_inT", (64, 128, 2048))
    din("conv_w", (128, NIT * 4))
    din("conv_b", (128, NIT))
    din("xpwT", (128, NIT * 160))
    din("dtpT", (128, NIT * 128))
    din("dt_b", (128, NIT))
    din("A_log", (128, NIT * NST))
    din("Dvec", (128, NIT))
    din("woutT", (NHT, 128, 4096))
    din("gate_w", (NFT, 128, 2048))
    din("up_w", (NFT, 128, 2048))
    din("down_w", (NHT, 128, 6144))
    din("gate_sc", (128, NFT))
    din("up_sc", (128, NFT))
    din("down_sc", (128, NHT))
    din("c_fm", (128, NHT))
    din("wmod", (NHT, 128, 12288))
    din("ident", (128, 128))
    din("ones_col_bf", (128, 1), BF16)
    din("ones_row", (1, 128))
    din("ones_row_bf", (1, 128), BF16)
    din("sel", (32, 32 * 128), BF16)
    din("eps_col", (128, 1))
    din("nh_col", (128, 1))
    out_d = nc.dram_tensor("out", [TOK, H], F32, kind="ExternalOutput").ap()
    dbg_d = {}
    for nm, shape, dt_ in debug:
        dbg_d[nm] = nc.dram_tensor("dbg_" + nm, list(shape), dt_, kind="ExternalOutput").ap()

    ctx = ExitStack()

    def sb(name, shape, dtype):
        return ctx.enter_context(nc.sbuf_tensor(name, list(shape), dtype))

    identt = sb("identt", (128, 128), F32)
    maskt = sb("maskt", (128, 1), F32)
    onescb = sb("onescb", (128, 1), BF16)
    onesr = sb("onesr", (1, 128), F32)
    onesrb = sb("onesrb", (1, 128), BF16)
    epsc = sb("epsc", (128, 1), F32)
    nhc = sb("nhc", (128, 1), F32)
    mod = sb("mod", (128, 96), F32)
    mtmp = sb("mtmp", (128, 64), F32)
    c_sb = sb("c_sb", (128, NHT), F32)
    c_bf = sb("c_bf", (128, NHT), BF16)
    convw = sb("convw", (128, NIT * 4), F32)
    convb = sb("convb", (128, NIT), F32)
    dtb = sb("dtb", (128, NIT), F32)
    Aneg = sb("Aneg", (128, NIT * NST), F32)
    Dv = sb("Dv", (128, NIT), F32)
    gsc = sb("gsc", (128, NFT), F32)
    usc = sb("usc", (128, NFT), F32)
    dscgb = sb("dscgb", (128, NHT), F32)
    xpw_sb = sb("xpw_sb", (128, NIT * 160), BF16)
    dtp_sb = sb("dtp_sb", (128, NIT * 128), BF16)
    ts_T = sb("ts_T", (128, 2 * TOK), BF16)
    bc_T = sb("bc_T", (32, 2 * TOK), BF16)
    ssm_sb = sb("ssm_sb", (128, 160), F32)
    carry = sb("carry", (128, NIT * NST), F32)
    halo = sb("halo", (128, NIT * 4), BF16)
    halo2 = sb("halo2", (128, NIT * 4), BF16)
    hin_t = sb("hin_t", (128, TOK + 4), BF16)
    cacc = sb("cacc", (128, TOK), F32)
    dA_t = [sb(f"dA{i}", (128, TOK), F32) for i in range(2)]
    dBx_t = sb("dBx_t", (128, TOK), BF16)
    h_t = sb("h_t", (128, TOK), BF16)
    yy_t = sb("yy_t", (128, TOK), BF16)
    Bb = sb("Bb", (128, TOK), BF16)
    Cb = sb("Cb", (128, TOK), BF16)
    sgt = sb("sgt", (128, TOK), BF16)
    sut = sb("sut", (128, 512), BF16)
    xsq = sb("xsq", (128, 512), BF16)
    srow_a = sb("srow_a", (1, 1024), F32)
    srow_b = sb("srow_b", (1, 1024), F32)
    strow2 = sb("strow2", (128, 10), F32)
    outsl = sb("outsl", (128, 2048), F32)
    p9f = sb("p9f", (128, 128), F32)
    xrow = sb("xrow", (128, 2048), F32)

    PB = 120 * KB
    pool = sb("pool", (128, PB), U8)
    pool_addr = nc.lookup_mloc(pool).addr
    _pvn = [0]

    def pv(off_b, dtype, cols, parts=128):
        isz = {F32: 4, BF16: 2, U8: 1}[dtype]
        assert off_b % isz == 0 and off_b + cols * isz <= PB, (off_b, cols)
        _pvn[0] += 1
        return nc.alloc_sbuf_tensor_at(f"pv{_pvn[0]}", [parts, cols], dtype,
                                       offset=pool_addr + off_b)

    wmst = [pv(i * 24 * KB, F32, 6144) for i in range(2)]
    xrow2 = pv(48 * KB, F32, 2048)
    wmsl = [pv(48 * KB + i * 12 * KB, BF16, 6144) for i in range(2)]
    h1_own = pv(0, BF16, NHT * TOK)
    x2 = pv(0, BF16, NHT * TOK)
    h1_pre = pv(32 * KB, BF16, NHT * 512)
    wslp_st = [pv(i * 8 * KB, F32, 2048) for i in range(2)]
    wslp = [pv(16 * KB + i * 4 * KB, BF16, 2048) for i in range(2)]
    dt4 = pv(32 * KB, BF16, 2 * TOK)
    w4 = pv(36 * KB, BF16, 2 * TOK)
    sel_sb = pv(40 * KB, BF16, 32 * 128, parts=32)
    wslo_st = [pv(32 * KB + i * 8 * KB, F32, 2048) for i in range(2)]
    wslo = [pv(48 * KB + i * 4 * KB, BF16, 2048) for i in range(2)]
    y4 = pv(48 * KB, F32, 2 * TOK)
    u_pre = pv(56 * KB, BF16, NIT * TOK)
    u_y = pv(56 * KB, BF16, NIT * TOK)
    wo_st = pv(32 * KB, F32, 4096)
    wosl = [nc.alloc_sbuf_tensor_at("wosl0", [128, 4096], BF16,
                                    offset=nc.lookup_mloc(xpw_sb).addr),
            nc.alloc_sbuf_tensor_at("wosl1", [128, 4096], BF16,
                                    offset=nc.lookup_mloc(dtp_sb).addr)]
    h2 = pv(32 * KB, BF16, NHT * TOK)
    dacc = pv(64 * KB, F32, NHT * 512)
    fuse = nc.alloc_sbuf_tensor_at("fuse", [128, 8 * 512], BF16,
                                   offset=nc.lookup_mloc(xpw_sb).addr)
    gu_st = pv(96 * KB, F32, 2048)
    dn_st = pv(104 * KB, F32, 1024)
    d8sl = [pv(108 * KB + i * 1 * KB, U8, 1024) for i in range(2)]
    g8sl = [pv(112 * KB + i * 2 * KB, U8, 2048) for i in range(2)]
    u8sl = [pv(116 * KB + i * 2 * KB, U8, 2048) for i in range(2)]

    def psd(name, shape):
        return ctx.enter_context(nc.psum_tensor(name, list(shape), F32))

    pmm = [psd(f"pmm{i}", (128, 512)) for i in range(4)]
    ptr = [psd(f"ptr{i}", (128, 512)) for i in range(2)]
    pst = [psd(f"pst{i}", (1, 512)) for i in range(2)]
    pmod = pmm[0]

    class Sem:
        def __init__(self, name):
            self.h = ctx.enter_context(nc.semaphore(name))
            self.n = 0

    dx, dwg, smm, svv, sss, sst = (Sem(s) for s in ("dx", "dwg", "mm", "vv", "ss", "st"))
    dwgb, dwgd, dxb = (Sem(s) for s in ("dwgb", "dwgd", "dxb"))
    dwg2 = [dwg, dwgb]
    dx2 = [dx, dxb]
    bufs = {k: [] for k in ("sync", "gpsimd", "tensor", "vector", "scalar")}

    def mkemit(eng):
        def op(meth, *a, inc=None, amt=1, **kw):
            if inc is not None:
                inc.n += amt

                def f(e, a=a, kw=kw, s=inc, amt=amt, meth=meth):
                    getattr(e, meth)(*a, **kw).then_inc(s.h, amt)
            else:
                def f(e, a=a, kw=kw, meth=meth):
                    getattr(e, meth)(*a, **kw)
            bufs[eng].append(f)
        return op

    T, V, S, SY, G = (mkemit(e) for e in ("tensor", "vector", "scalar", "sync", "gpsimd"))

    def wait(eng, sem, val=None):
        v = sem.n if val is None else val
        assert v <= sem.n, (eng, v, sem.n)
        bufs[eng].append(lambda e, s=sem.h, v=v: e.wait_ge(s, v))

    plast = {}

    def psum_reuse(eng, key):
        if key in plast:
            s, cnt = plast[key]
            wait(eng, s, cnt)

    def psum_consumed(key, sem):
        plast[key] = (sem, sem.n)

    def dbg_dump(nm, src_ap):
        if nm in dbg_d:
            wait("sync", svv, svv.n)
            wait("sync", sss, sss.n)
            wait("sync", smm, smm.n)
            SY("dma_start", out=dbg_d[nm], in_=src_ap, inc=sst, amt=16)
            for eng in ("vector", "scalar", "tensor", "gpsimd"):
                wait(eng, sst, sst.n)

    SY("dma_start", out=identt[:], in_=d["ident"], inc=dx, amt=16)
    SY("dma_start", out=maskt[:], in_=d["maskv"], inc=dx, amt=16)
    SY("dma_start", out=onescb[:], in_=d["ones_col_bf"], inc=dx, amt=16)
    SY("dma_start", out=onesr[:], in_=d["ones_row"], inc=dx, amt=16)
    SY("dma_start", out=onesrb[:], in_=d["ones_row_bf"], inc=dx, amt=16)
    SY("dma_start", out=epsc[:], in_=d["eps_col"], inc=dx, amt=16)
    SY("dma_start", out=nhc[:], in_=d["nh_col"], inc=dx, amt=16)
    SY("dma_start", out=c_sb[:], in_=d["c_fm"], inc=dx, amt=16)
    for nm, dst in (("conv_w", convw), ("conv_b", convb), ("dt_b", dtb),
                    ("A_log", Aneg), ("Dvec", Dv), ("gate_sc", gsc), ("up_sc", usc),
                    ("down_sc", dscgb)):
        SY("dma_start", out=dst[:], in_=d[nm], inc=dx, amt=16)
    xpw_st = pv(0, F32, NIT * 160)
    dtp_st = pv(24 * KB, F32, NIT * 128)
    SY("dma_start", out=xpw_st[:], in_=d["xpwT"], inc=dwg2[0], amt=16)
    SY("dma_start", out=dtp_st[:], in_=d["dtpT"], inc=dwg2[1], amt=16)
    wait("scalar", dwg2[0], dwg2[0].n)
    wait("scalar", dwg2[1], dwg2[1].n)
    for c0 in range(0, NIT * 160, 1280):
        S("activation", xpw_sb[:, c0:c0 + 1280], xpw_st[:, c0:c0 + 1280], ACT.Identity)
    for c0 in range(0, NIT * 128, 2048):
        S("activation", dtp_sb[:, c0:c0 + 2048], dtp_st[:, c0:c0 + 2048], ACT.Identity)
    S("drain")
    S("nop", inc=sss)
    XPW_SS = sss.n
    CONST_DX, CONST_DWG = dx.n, (dwg2[0].n, dwg2[1].n)

    wait("scalar", dx, CONST_DX)
    S("activation", Aneg[:], Aneg[:], ACT.Exp)
    S("activation", c_bf[:], c_sb[:], ACT.Silu, inc=sss)
    wait("vector", sss, sss.n)
    V("tensor_scalar", Aneg[:], Aneg[:], -1.0, None, ALU.mult)
    V("memset", mod[:], 0.0, inc=svv)

    WMC = {}
    WMM = {}
    for hs in range(2 * NHT):
        ht, hf = hs // 2, hs % 2
        kb = hs % 2
        if hs >= 2:
            wait("sync", sss, WMC[hs - 2])
        else:
            wait("sync", sss, XPW_SS)
        SY("dma_start", out=wmst[kb][:], in_=d["wmod"][ht][:, hf * 6144:(hf + 1) * 6144],
           inc=dwg2[kb], amt=16)
        wait("scalar", dwg2[kb], dwg2[kb].n)
        if hs == 0:
            wait("scalar", sss, XPW_SS)
        if hs >= 2:
            wait("scalar", smm, WMM[hs - 2])
        for c0 in range(0, 6144, 2048):
            S("activation", wmsl[kb][:, c0:c0 + 2048], wmst[kb][:, c0:c0 + 2048],
              ACT.Identity, inc=sss if c0 == 4096 else None)
        WMC[hs] = sss.n
        wait("tensor", sss, sss.n)
        wait("tensor", svv, svv.n)
        for mt in range(48):
            T("matmul", pmod[:, hf * 48 + mt: hf * 48 + mt + 1],
              wmsl[kb][:, mt * 128:(mt + 1) * 128],
              c_bf[:, ht:ht + 1], start=True, stop=True, inc=smm if mt == 47 else None)
        WMM[hs] = smm.n
        if hf == 1:
            wait("vector", smm, smm.n)
            V("tensor_tensor", mod[:], mod[:], pmod[:, 0:96], ALU.add, inc=svv)
    psum_consumed("pmm0", svv)
    dbg_dump("modraw", mod[:])

    wait("vector", svv, svv.n)
    V("tensor_scalar", mod[:, CA1:CA1 + 16], mod[:, CA1:CA1 + 16], 1.0, None, ALU.add)
    V("tensor_scalar", mod[:, CB1:CB1 + 16], mod[:, CB1:CB1 + 16], 1.0, None, ALU.add)
    for c0 in (GA, GB):
        V("tensor_scalar", mtmp[:, 0:16], mod[:, c0:c0 + 16], 0.0, None, ALU.is_ge)
        V("drain")
        V("tensor_scalar", mtmp[:, 16:32], mtmp[:, 0:16], 2.0, -1.0, ALU.mult, ALU.add)
        V("tensor_scalar", mtmp[:, 32:48], mod[:, c0:c0 + 16], -1.0, None, ALU.mult)
        V("drain")
        V("tensor_tensor", mtmp[:, 48:64], mod[:, c0:c0 + 16], mtmp[:, 32:48], ALU.max)
        V("drain")
        V("tensor_scalar", mtmp[:, 48:64], mtmp[:, 48:64], 0.25, None, ALU.max)
        V("drain")
        V("tensor_tensor", mod[:, c0:c0 + 16], mtmp[:, 16:32], mtmp[:, 48:64], ALU.mult)
        V("drain")
    V("tensor_tensor", dscgb[:], dscgb[:], mod[:, GB:GB + 16], ALU.mult, inc=svv)
    MOD_V = svv.n
    dbg_dump("mod", mod[:])

    LNS = {}
    LNQ = [0]

    def ln_tile(xd, lt, h1v, stride):
        q = LNQ[0] & 1
        LNQ[0] += 1
        xr = xrow if q == 0 else xrow2
        if q in LNS:
            wait("sync", smm, LNS[q])
        SY("dma_start", out=xr[:], in_=xd[lt * 128:(lt + 1) * 128, :], inc=dx2[q], amt=16)
        wait("vector", dx2[q], dx2[q].n)
        V("tensor_reduce", strow2[:, 0:1], xr[:], AXX, ALU.add)
        V("drain")
        V("tensor_scalar", strow2[:, 1:2], strow2[:, 0:1], 1.0 / 2048.0, None, ALU.mult, inc=svv)
        wait("scalar", svv, svv.n)
        S("activation", outsl[:], xr[:], ACT.Square, inc=sss)
        wait("vector", sss, sss.n)
        V("tensor_reduce", strow2[:, 2:3], outsl[:], AXX, ALU.add)
        V("drain")
        V("tensor_scalar", strow2[:, 3:4], strow2[:, 2:3], 1.0 / 2048.0, None, ALU.mult)
        V("tensor_tensor", strow2[:, 4:5], strow2[:, 1:2], strow2[:, 1:2], ALU.mult)
        V("drain")
        V("tensor_tensor", strow2[:, 5:6], strow2[:, 3:4], strow2[:, 4:5], ALU.subtract, inc=svv)
        wait("scalar", svv, svv.n)
        S("activation", strow2[:, 6:7], strow2[:, 5:6], ACT.Ln, bias=epsc[:])
        S("drain")
        S("activation", strow2[:, 7:8], strow2[:, 6:7], ACT.Exp, scale=nhc[:], inc=sss)
        wait("vector", sss, sss.n)
        V("tensor_tensor", strow2[:, 8:9], strow2[:, 1:2], strow2[:, 7:8], ALU.mult)
        V("drain")
        V("tensor_scalar", strow2[:, 9:10], strow2[:, 8:9], -1.0, None, ALU.mult)
        V("drain")
        V("tensor_scalar", xr[:], xr[:], strow2[:, 7:8], strow2[:, 9:10],
          ALU.mult, ALU.add, inc=svv)
        wait("tensor", svv, svv.n)
        nl = stride // 128
        for ht in range(NHT):
            pk = (ht // 4) % 2
            if ht % 4 == 0:
                psum_reuse("tensor", f"ptr{pk}")
            T("transpose", ptr[pk][:, (ht % 4) * 128:(ht % 4 + 1) * 128],
              xr[:, ht * 128:(ht + 1) * 128], identt[:],
              inc=smm if ht % 4 == 3 else None)
            if ht % 4 == 3:
                wait("vector", smm, smm.n)
                for hh in range(ht - 3, ht + 1):
                    col = hh * stride + (lt % nl) * 128
                    V("tensor_scalar", h1v[:, col:col + 128],
                      ptr[pk][:, (hh % 4) * 128:(hh % 4 + 1) * 128],
                      mod[:, CA1 + hh:CA1 + hh + 1], mod[:, SA + hh:SA + hh + 1],
                      ALU.mult, ALU.add, inc=svv if hh == ht else None)
                psum_consumed(f"ptr{pk}", svv)
        LNS[q] = smm.n

    IP = {"k": 0}

    def inproj_slab(pt, stg, slb):
        k = IP["k"]
        st_, wb = stg[k % 2], slb[k % 2]
        if k >= 2:
            wait("sync", sss, IP[(k - 2, "c")])
        SY("dma_start", out=st_[:], in_=d["w_inT"][pt], inc=dwg2[k % 2], amt=16)
        wait("scalar", dwg2[k % 2], dwg2[k % 2].n)
        if k >= 2:
            wait("scalar", smm, IP[k - 2])
        S("activation", wb[:], st_[:], ACT.Identity, inc=sss)
        IP[(k, "c")] = sss.n
        IP["k"] = k + 1
        return k, wb

    PCV = [0]
    wait("sync", smm, WMM[2 * NHT - 1])
    for lh in range(2):
        for lt4 in range(4):
            if lh == 0 and lt4 == 0:
                wait("vector", svv, MOD_V)
            ln_tile(d["x_pre"], lh * 4 + lt4, h1_pre, 512)
        H1PRE_V = svv.n
        if lh == 1:
            dbg_dump("h1p", h1_pre[:])
        if lh == 0:
            wait("sync", sss, WMC[2 * NHT - 1])
        for pt in range(32):
            it = pt
            k, wb = inproj_slab(pt, wslp_st, wslp)
            wait("tensor", sss, sss.n)
            if pt == 0:
                wait("tensor", svv, H1PRE_V)
            pb = pmm[2 + k % 2]
            psum_reuse("tensor", f"pmm{2 + k % 2}")
            for ht in range(NHT):
                T("matmul", pb[:], wb[:, ht * 128:(ht + 1) * 128],
                  h1_pre[:, ht * 512: ht * 512 + 512],
                  start=(ht == 0), stop=(ht == NHT - 1),
                  inc=smm if ht == NHT - 1 else None)
            IP[k] = smm.n
            wait("scalar", smm, smm.n)
            if PCV[0]:
                wait("scalar", svv, PCV[0])
            S("activation", hin_t[:, 4:516], pb[:], ACT.Identity, inc=sss)
            psum_consumed(f"pmm{2 + k % 2}", sss)
            wait("vector", sss, sss.n)
            if lh == 0:
                V("memset", hin_t[:, 0:4], 0.0)
                V("tensor_copy", halo2[:, it * 4:(it + 1) * 4], hin_t[:, 512:516])
            else:
                V("tensor_copy", hin_t[:, 0:4], halo2[:, it * 4:(it + 1) * 4])
                V("tensor_copy", halo[:, it * 4:(it + 1) * 4], hin_t[:, 512:516])
            V("drain")
            V("tensor_scalar", cacc[:, 0:512], hin_t[:, 4:516],
              convw[:, it * 4 + 3:it * 4 + 4], convb[:, it:it + 1], ALU.mult, ALU.add)
            for kk in (2, 1, 0):
                V("drain")
                V("scalar_tensor_tensor", cacc[:, 0:512], hin_t[:, 1 + kk:513 + kk],
                  convw[:, it * 4 + kk:it * 4 + kk + 1], cacc[:, 0:512], ALU.mult, ALU.add)
            V("drain")
            V("nop", inc=svv)
            PCV[0] = svv.n
            wait("scalar", svv, svv.n)
            S("activation", u_pre[:, it * TOK + lh * 512: it * TOK + lh * 512 + 512],
              cacc[:, 0:512], ACT.Silu, inc=sss)
    dbg_dump("u_pre", u_pre[:])

    wait("sync", smm, smm.n)
    SY("dma_start", out=sel_sb[:], in_=d["sel"], inc=dx, amt=16)

    def xproj(uv, off):
        wait("tensor", sss, sss.n)
        wait("tensor", dwg2[0], CONST_DWG[0])
        wait("tensor", dwg2[1], CONST_DWG[1])
        XPT = [0]
        for lt in range(NLT):
            pb = pmm[lt % 2]
            psum_reuse("tensor", f"pmm{lt % 2}")
            for it in range(NIT):
                T("matmul", pb[:, 0:160],
                  uv[:, it * TOK + lt * 128: it * TOK + lt * 128 + 128],
                  xpw_sb[:, it * 160:(it + 1) * 160],
                  start=(it == 0), stop=(it == NIT - 1),
                  inc=smm if it == NIT - 1 else None)
            wait("scalar", smm, smm.n)
            if XPT[0]:
                wait("scalar", smm, XPT[0])
            S("activation", ssm_sb[:], pb[:, 0:160], ACT.Identity, inc=sss)
            psum_consumed(f"pmm{lt % 2}", sss)
            wait("tensor", sss, sss.n)
            pk = lt % 2
            psum_reuse("tensor", f"ptr{pk}")
            T("transpose", ptr[pk][:, 0:128], ssm_sb[:, 0:128], identt[:])
            T("transpose", ptr[pk][0:32, 128:256], ssm_sb[:, 128:160], identt[:], inc=smm)
            XPT[0] = smm.n
            if off == TOK and lt == 3:
                dbg_dump("ssm3", ssm_sb[:])
            wait("vector", smm, smm.n)
            V("tensor_copy", ts_T[:, off + lt * 128: off + lt * 128 + 128], ptr[pk][:, 0:128])
            V("tensor_copy", bc_T[0:32, off + lt * 128: off + lt * 128 + 128],
              ptr[pk][0:32, 128:256], inc=svv)
            psum_consumed(f"ptr{pk}", svv)

    xproj(u_pre, 0)
    dbg_dump("ts_pre", ts_T[:, 0:TOK])
    dbg_dump("bc_pre", bc_T[:, 0:TOK])

    def scan_phase(uv, off, own):
        DAF = {}
        PW2 = [0]
        if own:
            wait("vector", svv, svv.n)
            V("tensor_scalar", carry[:], carry[:], maskt[:], None, ALU.mult)
            V("drain")
            V("nop", inc=svv)
        first = True
        BCV = [0]
        for g in range(NIT // 2):
            for j in range(2):
                it = g * 2 + j
                for lhh in range(2):
                    psum_reuse("tensor", f"pmm{2 + lhh}")
                    if first:
                        wait("tensor", svv, svv.n)
                        wait("tensor", dwg2[0], CONST_DWG[0])
                        wait("tensor", dwg2[1], CONST_DWG[1])
                        wait("tensor", dx, dx.n)
                        first = False
                    T("matmul", pmm[2 + lhh][:], dtp_sb[:, it * 128:(it + 1) * 128],
                      ts_T[:, off + lhh * 512: off + lhh * 512 + 512],
                      start=True, stop=True, inc=smm if lhh == 1 else None)
                wait("scalar", smm, smm.n)
                if PW2[0]:
                    wait("scalar", svv, PW2[0])
                dtv = dt4[:, j * TOK:(j + 1) * TOK]
                S("drain")
                S("activation", dtv[:, 0:512], pmm[2][:], ACT.Exp, bias=dtb[:, it:it + 1])
                S("activation", dtv[:, 512:1024], pmm[3][:], ACT.Exp, bias=dtb[:, it:it + 1],
                  inc=sss)
                psum_consumed("pmm2", sss)
                psum_consumed("pmm3", sss)
                S("drain")
                S("activation", dtv[:], dtv[:], ACT.Ln, bias=1.0, inc=sss)
                wait("vector", sss, sss.n)
                V("tensor_tensor", w4[:, j * TOK:(j + 1) * TOK], dtv[:],
                  uv[:, it * TOK:(it + 1) * TOK], ALU.mult)
                V("drain")
                V("nop", inc=svv)
            W2_V = svv.n
            PW2[0] = W2_V
            if not own and g == 0:
                dbg_dump("dt4g0", dt4[:])
                dbg_dump("w4g0", w4[:])
            if own and g == 0:
                dbg_dump("o_dt4", dt4[:])
                dbg_dump("o_w4", w4[:])
                dbg_dump("o_ts", ts_T[:])
                dbg_dump("o_bcT", bc_T[:])
                dbg_dump("o_u0", uv[:, 0:2 * TOK])
            for n in range(NST):
                psum_reuse("tensor", "ptr0")
                psum_reuse("tensor", "ptr1")
                T("matmul", ptr[0][:], sel_sb[:, n * 128:(n + 1) * 128],
                  bc_T[0:32, off:off + 512], start=True, stop=True)
                T("matmul", ptr[1][:], sel_sb[:, n * 128:(n + 1) * 128],
                  bc_T[0:32, off + 512:off + 1024], start=True, stop=True, inc=smm)
                wait("scalar", smm, smm.n)
                if BCV[0]:
                    wait("scalar", svv, BCV[0])
                S("activation", Bb[:, 0:512], ptr[0][:], ACT.Identity)
                S("activation", Bb[:, 512:1024], ptr[1][:], ACT.Identity, inc=sss)
                psum_consumed("ptr0", sss)
                psum_consumed("ptr1", sss)
                if own:
                    psum_reuse("tensor", "ptr0")
                    psum_reuse("tensor", "ptr1")
                    T("matmul", ptr[0][:], sel_sb[:, (16 + n) * 128:(17 + n) * 128],
                      bc_T[0:32, off:off + 512], start=True, stop=True)
                    T("matmul", ptr[1][:], sel_sb[:, (16 + n) * 128:(17 + n) * 128],
                      bc_T[0:32, off + 512:off + 1024], start=True, stop=True, inc=smm)
                    wait("scalar", smm, smm.n)
                    S("activation", Cb[:, 0:512], ptr[0][:], ACT.Identity)
                    S("activation", Cb[:, 512:1024], ptr[1][:], ACT.Identity, inc=sss)
                    psum_consumed("ptr0", sss)
                    psum_consumed("ptr1", sss)
                for j in range(2):
                    it = g * 2 + j
                    nb = (n * 2 + j) % 2
                    if nb in DAF:
                        wait("scalar", svv, DAF[nb])
                    S("activation", dA_t[nb][:], dt4[:, j * TOK:(j + 1) * TOK], ACT.Exp,
                      scale=Aneg[:, it * NST + n: it * NST + n + 1], inc=sss)
                    wait("vector", sss, sss.n)
                    if n == 0 and j == 0:
                        wait("vector", svv, W2_V)
                    V("tensor_tensor", dBx_t[:], w4[:, j * TOK:(j + 1) * TOK], Bb[:], ALU.mult)
                    V("drain")
                    if own:
                        V("tensor_tensor_scan", h_t[:], dA_t[nb][:], dBx_t[:],
                          carry[:, it * NST + n: it * NST + n + 1], ALU.mult, ALU.add)
                        V("drain")
                        if n == 0:
                            V("tensor_tensor", y4[:, j * TOK:(j + 1) * TOK],
                              Cb[:], h_t[:], ALU.mult, inc=svv)
                        else:
                            V("tensor_tensor", yy_t[:], Cb[:], h_t[:], ALU.mult)
                            V("drain")
                            V("tensor_tensor", y4[:, j * TOK:(j + 1) * TOK],
                              y4[:, j * TOK:(j + 1) * TOK], yy_t[:], ALU.add, inc=svv)
                    else:
                        V("tensor_tensor_scan", h_t[:], dA_t[nb][:], dBx_t[:], 0.0,
                          ALU.mult, ALU.add)
                        V("drain")
                        V("tensor_copy", carry[:, it * NST + n: it * NST + n + 1],
                          h_t[:, TOK - 1:TOK], inc=svv)
                    if own and g == 0 and n == 0 and j == 0:
                        dbg_dump("o_Bb", Bb[:])
                        dbg_dump("o_dA", dA_t[nb][:])
                        dbg_dump("o_dBx", dBx_t[:])
                        dbg_dump("o_h", h_t[:])
                        dbg_dump("o_Cb", Cb[:])
                        dbg_dump("o_y4", y4[:])
                    if own and g == 0 and n == 1 and j == 0:
                        dbg_dump("o_y4n1", y4[:])
                    if not own and g == 0 and n == 0 and j == 0:
                        dbg_dump("dA00", dA_t[nb][:])
                        dbg_dump("dBx00", dBx_t[:])
                        dbg_dump("h00", h_t[:])
                        dbg_dump("Bb0", Bb[:])
                    DAF[nb] = svv.n
                BCV[0] = svv.n
            if own:
                for j in range(2):
                    it = g * 2 + j
                    V("drain")
                    V("scalar_tensor_tensor", uv[:, it * TOK:(it + 1) * TOK],
                      uv[:, it * TOK:(it + 1) * TOK], Dv[:, it:it + 1],
                      y4[:, j * TOK:(j + 1) * TOK], ALU.mult, ALU.add, inc=svv)
                if g == 0:
                    dbg_dump("o_fin0", uv[:, 0:TOK])
                    dbg_dump("o_y4f", y4[:])

    scan_phase(u_pre, 0, own=False)
    dbg_dump("carry", carry[:])

    wait("sync", svv, svv.n)
    for lt in range(NLT):
        ln_tile(d["x_own"], lt, h1_own, TOK)
    H1OWN_V = svv.n
    dbg_dump("h1o", h1_own[:])
    wait("sync", svv, svv.n)
    wait("scalar", svv, svv.n)
    for pt in range(32):
        it = pt
        k, wb = inproj_slab(pt, wslo_st, wslo)
        wait("tensor", sss, sss.n)
        if pt == 0:
            wait("tensor", svv, H1OWN_V)
        for lhh in range(2):
            pb = pmm[(k % 2) * 2 + lhh]
            psum_reuse("tensor", f"pmm{(k % 2) * 2 + lhh}")
            for ht in range(NHT):
                T("matmul", pb[:], wb[:, ht * 128:(ht + 1) * 128],
                  h1_own[:, ht * TOK + lhh * 512: ht * TOK + lhh * 512 + 512],
                  start=(ht == 0), stop=(ht == NHT - 1),
                  inc=smm if ht == NHT - 1 else None)
        IP[k] = smm.n
        wait("scalar", smm, smm.n)
        if PCV[0]:
            wait("scalar", svv, PCV[0])
        S("activation", hin_t[:, 4:516], pmm[(k % 2) * 2][:], ACT.Identity)
        S("activation", hin_t[:, 516:1028], pmm[(k % 2) * 2 + 1][:], ACT.Identity, inc=sss)
        psum_consumed(f"pmm{(k % 2) * 2}", sss)
        psum_consumed(f"pmm{(k % 2) * 2 + 1}", sss)
        wait("vector", sss, sss.n)
        V("tensor_scalar", hin_t[:, 0:4], halo[:, it * 4:(it + 1) * 4],
          maskt[:], None, ALU.mult)
        V("drain")
        V("tensor_scalar", cacc[:], hin_t[:, 4:4 + TOK],
          convw[:, it * 4 + 3:it * 4 + 4], convb[:, it:it + 1], ALU.mult, ALU.add)
        for kk in (2, 1, 0):
            V("drain")
            V("scalar_tensor_tensor", cacc[:], hin_t[:, 1 + kk:1 + kk + TOK],
              convw[:, it * 4 + kk:it * 4 + kk + 1], cacc[:], ALU.mult, ALU.add)
        V("drain")
        V("nop", inc=svv)
        PCV[0] = svv.n
        wait("scalar", svv, svv.n)
        S("activation", u_y[:, it * TOK:(it + 1) * TOK], cacc[:], ACT.Silu, inc=sss)
    dbg_dump("u_own", u_y[:])

    dbg_dump("xpw", xpw_sb[:])
    dbg_dump("dtp", dtp_sb[:])
    wait("sync", sss, IP[(95, "c")])
    SY("dma_start", out=sel_sb[:], in_=d["sel"], inc=dx, amt=16)
    xproj(u_y, TOK)
    dbg_dump("ts_own", ts_T[:, TOK:2 * TOK])
    dbg_dump("bc_own", bc_T[:, TOK:2 * TOK])
    scan_phase(u_y, TOK, own=True)
    dbg_dump("yraw", u_y[:])

    wait("sync", svv, svv.n)
    for pt in range(32, 64):
        it = pt - 32
        k, wb = inproj_slab(pt, wslo_st, wslo)
        wait("tensor", sss, sss.n)
        for lhh in range(2):
            pb = pmm[(k % 2) * 2 + lhh]
            psum_reuse("tensor", f"pmm{(k % 2) * 2 + lhh}")
            for ht in range(NHT):
                T("matmul", pb[:], wb[:, ht * 128:(ht + 1) * 128],
                  h1_own[:, ht * TOK + lhh * 512: ht * TOK + lhh * 512 + 512],
                  start=(ht == 0), stop=(ht == NHT - 1),
                  inc=smm if ht == NHT - 1 else None)
        IP[k] = smm.n
        wait("scalar", smm, smm.n)
        wait("scalar", svv, svv.n)
        S("activation", sgt[:, 0:512], pmm[(k % 2) * 2][:], ACT.Silu)
        S("activation", sgt[:, 512:1024], pmm[(k % 2) * 2 + 1][:], ACT.Silu, inc=sss)
        psum_consumed(f"pmm{(k % 2) * 2}", sss)
        psum_consumed(f"pmm{(k % 2) * 2 + 1}", sss)
        wait("vector", sss, sss.n)
        V("tensor_tensor", u_y[:, it * TOK:(it + 1) * TOK],
          u_y[:, it * TOK:(it + 1) * TOK], sgt[:], ALU.mult, inc=svv)
    YG_V = svv.n
    dbg_dump("ygated", u_y[:])

    WO = {}
    WOC = {}
    wait("sync", svv, svv.n)
    for ht in range(NHT):
        kb2 = ht % 2
        if ht >= 1:
            wait("sync", sss, WOC[ht - 1])
        SY("dma_start", out=wo_st[:], in_=d["woutT"][ht], inc=dwg2[0], amt=16)
        wait("scalar", dwg2[0], dwg2[0].n)
        if ht >= 2:
            wait("scalar", smm, WO[ht - 2])
        S("activation", wosl[kb2][:, 0:2048], wo_st[:, 0:2048], ACT.Identity)
        S("activation", wosl[kb2][:, 2048:4096], wo_st[:, 2048:4096], ACT.Identity, inc=sss)
        WOC[ht] = sss.n
        wait("tensor", sss, sss.n)
        if ht == 0:
            wait("tensor", svv, YG_V)
        for lhh in range(2):
            pb = pmm[kb2 * 2 + lhh]
            psum_reuse("tensor", f"pmm{kb2 * 2 + lhh}")
            for it in range(NIT):
                T("matmul", pb[:], wosl[kb2][:, it * 128:(it + 1) * 128],
                  u_y[:, it * TOK + lhh * 512: it * TOK + lhh * 512 + 512],
                  start=(it == 0), stop=(it == NIT - 1),
                  inc=smm if it == NIT - 1 else None)
        WO[ht] = smm.n
        wait("vector", smm, smm.n)
        for lhh in range(2):
            V("tensor_scalar", x2[:, ht * TOK + lhh * 512: ht * TOK + lhh * 512 + 512],
              pmm[kb2 * 2 + lhh][:], mod[:, GA + ht:GA + ht + 1], None, ALU.mult,
              inc=svv if lhh == 1 else None)
        psum_consumed(f"pmm{kb2 * 2}", svv)
        psum_consumed(f"pmm{kb2 * 2 + 1}", svv)

    for lt in range(NLT):
        wait("sync", smm, smm.n)
        SY("dma_start", out=xrow[:], in_=d["x_own"][lt * 128:(lt + 1) * 128, :], inc=dx, amt=16)
        wait("tensor", dx, dx.n)
        if lt == 0:
            wait("tensor", svv, svv.n)
        for ht in range(NHT):
            pk = (ht // 4) % 2
            if ht % 4 == 0:
                psum_reuse("tensor", f"ptr{pk}")
            T("transpose", ptr[pk][:, (ht % 4) * 128:(ht % 4 + 1) * 128],
              xrow[:, ht * 128:(ht + 1) * 128], identt[:],
              inc=smm if ht % 4 == 3 else None)
            if ht % 4 == 3:
                wait("vector", smm, smm.n)
                for hh in range(ht - 3, ht + 1):
                    col = hh * TOK + lt * 128
                    V("tensor_tensor", x2[:, col:col + 128], x2[:, col:col + 128],
                      ptr[pk][:, (hh % 4) * 128:(hh % 4 + 1) * 128], ALU.add,
                      inc=svv if hh == ht else None)
                psum_consumed(f"ptr{pk}", svv)
    X2_V = svv.n
    dbg_dump("x2", x2[:])

    for lhh in range(2):
        psum_reuse("tensor", "pst0")
        psum_reuse("tensor", "pst1")
        wait("tensor", svv, X2_V)
        wait("scalar", svv, X2_V)
        XQT = [0]
        for ht in range(NHT):
            col = ht * TOK + lhh * 512
            if XQT[0]:
                wait("scalar", smm, XQT[0])
            S("activation", xsq[:], x2[:, col:col + 512], ACT.Square, inc=sss)
            wait("tensor", sss, sss.n)
            T("matmul", pst[0][:], onescb[:], x2[:, col:col + 512],
              start=(ht == 0), stop=(ht == NHT - 1))
            T("matmul", pst[1][:], onescb[:], xsq[:],
              start=(ht == 0), stop=(ht == NHT - 1), inc=smm)
            XQT[0] = smm.n
        wait("vector", smm, smm.n)
        V("tensor_scalar", srow_a[0:1, 0:512], pst[0][0:1, :], 1.0 / 2048.0, None, ALU.mult)
        V("tensor_scalar", srow_a[0:1, 512:1024], pst[1][0:1, :], 1.0 / 2048.0, None, ALU.mult)
        V("drain")
        V("tensor_tensor", srow_b[0:1, 0:512], srow_a[0:1, 0:512], srow_a[0:1, 0:512],
          ALU.mult)
        V("drain")
        V("tensor_tensor", srow_b[0:1, 512:1024], srow_a[0:1, 512:1024], srow_b[0:1, 0:512],
          ALU.subtract, inc=svv)
        psum_consumed("pst0", svv)
        psum_consumed("pst1", svv)
        wait("scalar", svv, svv.n)
        S("activation", srow_b[0:1, 0:512], srow_b[0:1, 512:1024], ACT.Ln, bias=epsc[0:1, :])
        S("drain")
        S("activation", srow_a[0:1, 512:1024], srow_b[0:1, 0:512], ACT.Exp, scale=nhc[0:1, :],
          inc=sss)
        wait("vector", sss, sss.n)
        V("tensor_tensor", srow_b[0:1, 512:1024], srow_a[0:1, 0:512], srow_a[0:1, 512:1024],
          ALU.mult)
        V("drain")
        V("tensor_scalar", srow_b[0:1, 512:1024], srow_b[0:1, 512:1024], -1.0, None, ALU.mult,
          inc=svv)
        wait("tensor", svv, svv.n)
        psum_reuse("tensor", "ptr0")
        psum_reuse("tensor", "ptr1")
        T("matmul", ptr[0][:], onesr[:], srow_a[0:1, 512:1024], start=True, stop=True)
        T("matmul", ptr[1][:], onesr[:], srow_b[0:1, 512:1024], start=True, stop=True, inc=smm)
        wait("vector", smm, smm.n)
        for ht in range(NHT):
            col = ht * TOK + lhh * 512
            V("tensor_tensor", h2[:, col:col + 512], x2[:, col:col + 512], ptr[0][:], ALU.mult)
            V("drain")
            V("tensor_tensor", h2[:, col:col + 512], h2[:, col:col + 512], ptr[1][:], ALU.add)
            V("drain")
            V("tensor_scalar", h2[:, col:col + 512], h2[:, col:col + 512],
              mod[:, CB1 + ht:CB1 + ht + 1], mod[:, SB_ + ht:SB_ + ht + 1],
              ALU.mult, ALU.add, inc=svv if ht == NHT - 1 else None)
        psum_consumed("ptr0", svv)
        psum_consumed("ptr1", svv)
    H2_V = svv.n
    dbg_dump("h2", h2[:])

    GUF = {}
    DWF = {}
    wait("sync", smm, smm.n)
    wait("sync", svv, svv.n)
    OUT_N = [0]
    NG = 6
    for lhh in range(2):
        for grp in range(NG):
            for itg in range(8):
                it = grp * 8 + itg
                kb2 = itg % 2
                if ("gust",) in GUF:
                    wait("sync", svv, GUF[("gust",)])
                SY("dma_start", out=gu_st[:], in_=d["gate_w"][it], inc=dwg2[0], amt=16)
                wait("vector", dwg2[0], dwg2[0].n)
                if (kb2, "g") in GUF:
                    wait("vector", smm, GUF[(kb2, "g")])
                V("tensor_copy", g8sl[kb2].ap().bitcast(FP8), gu_st[:], inc=svv)
                wait("sync", svv, svv.n)
                SY("dma_start", out=gu_st[:], in_=d["up_w"][it], inc=dwg2[0], amt=16)
                wait("vector", dwg2[0], dwg2[0].n)
                if (kb2, "u") in GUF:
                    wait("vector", smm, GUF[(kb2, "u")])
                V("tensor_copy", u8sl[kb2].ap().bitcast(FP8), gu_st[:], inc=svv)
                GUF[("gust",)] = svv.n
                wait("tensor", svv, svv.n)
                if (lhh, grp, itg) == (0, 0, 0):
                    wait("tensor", svv, H2_V)
                pg, pu = pmm[kb2 * 2], pmm[kb2 * 2 + 1]
                psum_reuse("tensor", f"pmm{kb2 * 2}")
                psum_reuse("tensor", f"pmm{kb2 * 2 + 1}")
                for ht in range(NHT):
                    col = ht * TOK + lhh * 512
                    T("matmul", pg[:], g8sl[kb2].ap().bitcast(FP8)[:, ht * 128:(ht + 1) * 128],
                      h2[:, col:col + 512], start=(ht == 0), stop=(ht == NHT - 1))
                    T("matmul", pu[:], u8sl[kb2].ap().bitcast(FP8)[:, ht * 128:(ht + 1) * 128],
                      h2[:, col:col + 512], start=(ht == 0), stop=(ht == NHT - 1),
                      inc=smm if ht == NHT - 1 else None)
                GUF[(kb2, "g")] = smm.n
                GUF[(kb2, "u")] = smm.n
                wait("scalar", smm, smm.n)
                wait("scalar", svv, svv.n)
                S("activation", sgt[:, 0:512], pg[:], ACT.Silu, scale=gsc[:, it:it + 1])
                S("activation", sut[:, 0:512], pu[:], ACT.Identity, scale=usc[:, it:it + 1],
                  inc=sss)
                psum_consumed(f"pmm{kb2 * 2}", sss)
                psum_consumed(f"pmm{kb2 * 2 + 1}", sss)
                wait("vector", sss, sss.n)
                V("tensor_tensor", fuse[:, itg * 512:(itg + 1) * 512],
                  sgt[:, 0:512], sut[:, 0:512], ALU.mult, inc=svv)
            FUSE_V = svv.n
            if lhh == 0 and grp == 0:
                dbg_dump("fuse0", fuse[:])
            for ht in range(NHT):
                kb2 = ht % 2
                if ("dnst",) in DWF:
                    wait("sync", svv, DWF[("dnst",)])
                SY("dma_start", out=dn_st[:],
                   in_=d["down_w"][ht][:, grp * 1024:(grp + 1) * 1024], inc=dwgd, amt=16)
                wait("vector", dwgd, dwgd.n)
                if (kb2, "d") in DWF:
                    wait("vector", smm, DWF[(kb2, "d")])
                V("tensor_copy", d8sl[kb2].ap().bitcast(FP8), dn_st[:], inc=svv)
                DWF[("dnst",)] = svv.n
                wait("tensor", svv, svv.n)
                if ht == 0:
                    wait("tensor", svv, FUSE_V)
                pb = pmm[kb2 * 2]
                psum_reuse("tensor", f"pmm{kb2 * 2}")
                for itg in range(8):
                    T("matmul", pb[:], d8sl[kb2].ap().bitcast(FP8)[:, itg * 128:(itg + 1) * 128],
                      fuse[:, itg * 512:(itg + 1) * 512],
                      start=(itg == 0), stop=(itg == 7),
                      inc=smm if itg == 7 else None)
                DWF[(kb2, "d")] = smm.n
                wait("vector", smm, smm.n)
                dcol = ht * 512
                if grp == 0:
                    V("tensor_copy", dacc[:, dcol:dcol + 512], pb[:], inc=svv)
                else:
                    V("tensor_tensor", dacc[:, dcol:dcol + 512], dacc[:, dcol:dcol + 512],
                      pb[:], ALU.add, inc=svv)
                psum_consumed(f"pmm{kb2 * 2}", svv)
        DACC_V = svv.n
        if lhh == 0:
            dbg_dump("dacc0", dacc[:])
        for lt in range(4):
            if OUT_N[0] > 0:
                wait("vector", sst, OUT_N[0])
            for ht in range(NHT):
                pk = ht % 2
                if lt == 0 and ht < 2:
                    wait("vector", svv, DACC_V)
                V("scalar_tensor_tensor", p9f[:],
                  dacc[:, ht * 512 + lt * 128: ht * 512 + lt * 128 + 128],
                  dscgb[:, ht:ht + 1],
                  x2[:, ht * TOK + lhh * 512 + lt * 128: ht * TOK + lhh * 512 + lt * 128 + 128],
                  ALU.mult, ALU.add, inc=svv)
                psum_reuse("tensor", f"ptr{pk}")
                wait("tensor", svv, svv.n)
                T("transpose", ptr[pk][:, 0:128], p9f[:], identt[:], inc=smm)
                wait("vector", smm, smm.n)
                V("tensor_copy", outsl[:, ht * 128:(ht + 1) * 128], ptr[pk][:, 0:128],
                  inc=svv)
                psum_consumed(f"ptr{pk}", svv)
            wait("sync", svv, svv.n)
            row0 = lhh * 512 + lt * 128
            SY("dma_start", out=out_d[row0:row0 + 128, :], in_=outsl[:], inc=sst, amt=16)
            OUT_N[0] = sst.n
    wait("sync", sst, sst.n)

    with nc.Block() as block:
        @block.sync
        def _(e):
            for fn in bufs["sync"]:
                fn(e)

        @block.gpsimd
        def _(e):
            for fn in bufs["gpsimd"]:
                fn(e)

        @block.tensor
        def _(e):
            for fn in bufs["tensor"]:
                fn(e)

        @block.vector
        def _(e):
            for fn in bufs["vector"]:
                fn(e)

        @block.scalar
        def _(e):
            for fn in bufs["scalar"]:
                fn(e)

    ctx.close()
    return nc, bufs


def _emit(nc, bufs):
    with nc.Block() as block:
        @block.sync
        def _(e):
            for f in bufs["sync"]:
                f(e)

        @block.gpsimd
        def _(e):
            for f in bufs["gpsimd"]:
                f(e)

        @block.tensor
        def _(e):
            for f in bufs["tensor"]:
                f(e)

        @block.vector
        def _(e):
            for f in bufs["vector"]:
                f(e)

        @block.scalar
        def _(e):
            for f in bufs["scalar"]:
                f(e)


def host_inputs(inputs):
    f = np.ascontiguousarray
    ipw = np.asarray(inputs["L0_attn_in_proj_w"])
    w_inT = f(ipw.T.reshape(16, 128, 64, 128).transpose(2, 1, 0, 3).reshape(64, 128, 2048))
    shared = dict(
        w_inT=w_inT,
        conv_w=f(np.asarray(inputs["L0_attn_conv_w"]).reshape(NIT, 128, 4)
                 .transpose(1, 0, 2).reshape(128, NIT * 4)),
        conv_b=f(np.asarray(inputs["L0_attn_conv_b"]).reshape(NIT, 128).T),
        xpwT=f(np.asarray(inputs["L0_attn_x_proj_w"]).T.reshape(NIT, 128, 160)
               .transpose(1, 0, 2).reshape(128, NIT * 160)),
        dtpT=f(np.asarray(inputs["L0_attn_dt_proj_w"]).T),
        dt_b=f(np.asarray(inputs["L0_attn_dt_proj_b"]).reshape(NIT, 128).T),
        A_log=f(np.asarray(inputs["L0_attn_A_log"]).reshape(NIT, 128, NST)
                .transpose(1, 0, 2).reshape(128, NIT * NST)),
        Dvec=f(np.asarray(inputs["L0_attn_D"]).reshape(NIT, 128).T),
        woutT=f(np.asarray(inputs["L0_attn_out_proj_w"]).T.reshape(NIT, 128, NHT, 128)
                .transpose(2, 1, 0, 3).reshape(NHT, 128, 4096)),
        gate_w=f(np.asarray(inputs["L0_ffn_gate_w"]).reshape(NHT, 128, NFT, 128)
                 .transpose(2, 1, 0, 3).reshape(NFT, 128, 2048)),
        up_w=f(np.asarray(inputs["L0_ffn_up_w"]).reshape(NHT, 128, NFT, 128)
               .transpose(2, 1, 0, 3).reshape(NFT, 128, 2048)),
        down_w=f(np.asarray(inputs["L0_ffn_down_w"]).reshape(NFT, 128, NHT, 128)
                 .transpose(2, 1, 0, 3).reshape(NHT, 128, 6144)),
        gate_sc=f(np.asarray(inputs["L0_ffn_gate_scale"]).reshape(NFT, 128).T),
        up_sc=f(np.asarray(inputs["L0_ffn_up_scale"]).reshape(NFT, 128).T),
        down_sc=f(np.asarray(inputs["L0_ffn_down_scale"]).reshape(NHT, 128).T),
        wmod=f(np.asarray(inputs["L0_wmod"]).reshape(NHT, 128, 12288)),
        ident=np.eye(128, dtype=np.float32),
        ones_col_bf=np.ones((128, 1), ml_dtypes.bfloat16),
        ones_row=np.ones((1, 128), np.float32),
        ones_row_bf=np.ones((1, 128), ml_dtypes.bfloat16),
        eps_col=np.full((128, 1), 1e-5, np.float32),
        nh_col=np.full((128, 1), -0.5, np.float32),
        sel=np.ascontiguousarray(
            np.tile(np.eye(32, dtype=ml_dtypes.bfloat16)[:, :, None], (1, 1, 128))
            .reshape(32, 32 * 128)),
    )
    x = np.asarray(inputs["L0_attn_x"])
    c = np.asarray(inputs["L0_c"])
    maps = []
    for core in range(8):
        b, half = core >> 1, core & 1
        m = dict(shared)
        m["x_own"] = f(x[b, half * TOK:(half + 1) * TOK])
        m["x_pre"] = f(x[b, (1 - half) * TOK:(2 - half) * TOK])
        m["maskv"] = np.full((128, 1), float(half), np.float32)
        m["c_fm"] = f(c[b].reshape(NHT, 128).T)
        maps.append(m)
    return maps


def kernel(**inputs):
    nc, bufs = build_program()
    maps = host_inputs(inputs)
    res = run_bass_kernel_spmd(nc, maps, core_ids=list(range(8)))
    out = np.empty((B, L, H), np.float32)
    for core in range(8):
        b, half = core >> 1, core & 1
        out[b, half * TOK:(half + 1) * TOK] = res.results[core]["out"]
    return out


# version 12 (final)
# speedup vs baseline: 1.1814x; 1.1814x over previous
from contextlib import ExitStack

import numpy as np
import ml_dtypes
import concourse.bass as bass
import concourse.mybir as mybir
from concourse.bass_utils import run_bass_kernel_spmd

F32 = mybir.dt.float32
BF16 = mybir.dt.bfloat16
FP8 = mybir.dt.float8e4
U8 = mybir.dt.uint8
ALU = mybir.AluOpType
ACT = mybir.ActivationFunctionType
AXX = mybir.AxisListType.X

B, L, H, I, NST = 4, 2048, 2048, 4096, 16
TOK = 1024
NHT, NIT, NFT, NLT = 16, 32, 48, 8
KB = 1024
SA, CA1, GA, SB_, CB1, GB = 0, 16, 32, 48, 64, 80


def build_program(debug=()):
    nc = bass.Bass("TRN2", target_bir_lowering=False)
    d = {}

    def din(name, shape, dtype=F32):
        d[name] = nc.dram_tensor(name, list(shape), dtype, kind="ExternalInput").ap()

    din("x_own", (TOK, H))
    din("x_pre", (TOK, H))
    din("maskv", (128, 1))
    din("w_inT", (64, 128, 2048))
    din("conv_w", (128, NIT * 4))
    din("conv_b", (128, NIT))
    din("xpwT", (128, NIT * 160))
    din("dtpT", (128, NIT * 128))
    din("dt_b", (128, NIT))
    din("A_log", (128, NIT * NST))
    din("Dvec", (128, NIT))
    din("woutT", (NHT, 128, 4096))
    din("gate_w", (NFT, 128, 2048))
    din("up_w", (NFT, 128, 2048))
    din("down_w", (NHT, 128, 6144))
    din("gate_sc", (128, NFT))
    din("up_sc", (128, NFT))
    din("down_sc", (128, NHT))
    din("c_fm", (128, NHT))
    din("wmod", (NHT, 128, 12288))
    din("ident", (128, 128))
    din("ones_col_bf", (128, 1), BF16)
    din("ones_row", (1, 128))
    din("ones_row_bf", (1, 128), BF16)
    din("sel", (32, 32 * 128), BF16)
    din("eps_col", (128, 1))
    din("nh_col", (128, 1))
    out_d = nc.dram_tensor("out", [TOK, H], F32, kind="ExternalOutput").ap()
    dbg_d = {}
    for nm, shape, dt_ in debug:
        dbg_d[nm] = nc.dram_tensor("dbg_" + nm, list(shape), dt_, kind="ExternalOutput").ap()

    ctx = ExitStack()

    def sb(name, shape, dtype):
        return ctx.enter_context(nc.sbuf_tensor(name, list(shape), dtype))

    identt = sb("identt", (128, 128), F32)
    maskt = sb("maskt", (128, 1), F32)
    onescb = sb("onescb", (128, 1), BF16)
    onesr = sb("onesr", (1, 128), F32)
    onesrb = sb("onesrb", (1, 128), BF16)
    epsc = sb("epsc", (128, 1), F32)
    nhc = sb("nhc", (128, 1), F32)
    mod = sb("mod", (128, 96), F32)
    mtmp = sb("mtmp", (128, 64), F32)
    c_sb = sb("c_sb", (128, NHT), F32)
    c_bf = sb("c_bf", (128, NHT), BF16)
    convw = sb("convw", (128, NIT * 4), F32)
    convb = sb("convb", (128, NIT), F32)
    dtb = sb("dtb", (128, NIT), F32)
    Aneg = sb("Aneg", (128, NIT * NST), F32)
    Dv = sb("Dv", (128, NIT), F32)
    gsc = sb("gsc", (128, NFT), F32)
    usc = sb("usc", (128, NFT), F32)
    dscgb = sb("dscgb", (128, NHT), F32)
    xpw_sb = sb("xpw_sb", (128, NIT * 160), BF16)
    dtp_sb = sb("dtp_sb", (128, NIT * 128), BF16)
    ts_T = sb("ts_T", (128, 2 * TOK), BF16)
    bc_T = sb("bc_T", (32, 2 * TOK), BF16)
    ssm_sb = sb("ssm_sb", (128, 160), F32)
    carry = sb("carry", (128, NIT * NST), F32)
    halo = sb("halo", (128, NIT * 4), BF16)
    halo2 = sb("halo2", (128, NIT * 4), BF16)
    hin_t = sb("hin_t", (128, TOK + 4), BF16)
    cacc = sb("cacc", (128, TOK), F32)
    dA_t = [sb(f"dA{i}", (128, TOK), F32) for i in range(2)]
    dBx_t = sb("dBx_t", (128, TOK), BF16)
    h_t = sb("h_t", (128, TOK), BF16)
    yy_t = sb("yy_t", (128, TOK), BF16)
    Bb = sb("Bb", (128, TOK), BF16)
    Cb = sb("Cb", (128, TOK), BF16)
    sgt = sb("sgt", (128, TOK), BF16)
    sut = sb("sut", (128, 512), BF16)
    xsq = sb("xsq", (128, 512), BF16)
    srow_a = sb("srow_a", (1, 1024), F32)
    srow_b = sb("srow_b", (1, 1024), F32)
    strow2 = sb("strow2", (128, 10), F32)
    outsl = sb("outsl", (128, 2048), F32)
    p9f = sb("p9f", (128, 128), F32)
    xrow = sb("xrow", (128, 2048), F32)

    PB = 120 * KB
    pool = sb("pool", (128, PB), U8)
    pool_addr = nc.lookup_mloc(pool).addr
    _pvn = [0]

    def pv(off_b, dtype, cols, parts=128):
        isz = {F32: 4, BF16: 2, U8: 1}[dtype]
        assert off_b % isz == 0 and off_b + cols * isz <= PB, (off_b, cols)
        _pvn[0] += 1
        return nc.alloc_sbuf_tensor_at(f"pv{_pvn[0]}", [parts, cols], dtype,
                                       offset=pool_addr + off_b)

    Bb2 = nc.alloc_sbuf_tensor_at("Bb2", [128, TOK], BF16,
                                  offset=nc.lookup_mloc(outsl).addr)
    Cb2 = nc.alloc_sbuf_tensor_at("Cb2", [128, TOK], BF16,
                                  offset=nc.lookup_mloc(outsl).addr + 2 * TOK)
    Bbv, Cbv = [Bb, Bb2], [Cb, Cb2]

    wmst = [pv(i * 24 * KB, F32, 6144) for i in range(2)]
    xrow2 = pv(48 * KB, F32, 2048)
    wmsl = [pv(48 * KB + i * 12 * KB, BF16, 6144) for i in range(2)]
    h1_own = pv(0, BF16, NHT * TOK)
    x2 = pv(0, BF16, NHT * TOK)
    h1_pre = pv(32 * KB, BF16, NHT * 512)
    wslp_st = [pv(i * 8 * KB, F32, 2048) for i in range(2)]
    wslp = [pv(16 * KB + i * 4 * KB, BF16, 2048) for i in range(2)]
    dt4 = pv(32 * KB, BF16, 2 * TOK)
    w4 = pv(36 * KB, BF16, 2 * TOK)
    sel_sb = pv(40 * KB, BF16, 32 * 128, parts=32)
    wslo_st = [pv(32 * KB + i * 8 * KB, F32, 2048) for i in range(2)]
    wslo = [pv(48 * KB + i * 4 * KB, BF16, 2048) for i in range(2)]
    y4 = pv(48 * KB, F32, 2 * TOK)
    u_pre = pv(56 * KB, BF16, NIT * TOK)
    u_y = pv(56 * KB, BF16, NIT * TOK)
    wo_st = pv(32 * KB, F32, 4096)
    wosl = [nc.alloc_sbuf_tensor_at("wosl0", [128, 4096], BF16,
                                    offset=nc.lookup_mloc(xpw_sb).addr),
            nc.alloc_sbuf_tensor_at("wosl1", [128, 4096], BF16,
                                    offset=nc.lookup_mloc(dtp_sb).addr)]
    h2 = pv(32 * KB, BF16, NHT * TOK)
    dacc = pv(64 * KB, F32, NHT * 512)
    fuse = nc.alloc_sbuf_tensor_at("fuse", [128, 8 * 512], BF16,
                                   offset=nc.lookup_mloc(xpw_sb).addr)
    gu_st = pv(96 * KB, F32, 2048)
    dn_st = pv(104 * KB, F32, 1024)
    d8sl = [pv(108 * KB + i * 1 * KB, U8, 1024) for i in range(2)]
    g8sl = [pv(112 * KB + i * 2 * KB, U8, 2048) for i in range(2)]
    u8sl = [pv(116 * KB + i * 2 * KB, U8, 2048) for i in range(2)]

    def psd(name, shape):
        return ctx.enter_context(nc.psum_tensor(name, list(shape), F32))

    pmm = [psd(f"pmm{i}", (128, 512)) for i in range(4)]
    ptr = [psd(f"ptr{i}", (128, 512)) for i in range(2)]
    pst = [psd(f"pst{i}", (1, 512)) for i in range(2)]
    pmod = pmm[0]

    class Sem:
        def __init__(self, name):
            self.h = ctx.enter_context(nc.semaphore(name))
            self.n = 0

    dx, dwg, smm, svv, sss, sst = (Sem(s) for s in ("dx", "dwg", "mm", "vv", "ss", "st"))
    dwgb, dwgd, dxb = (Sem(s) for s in ("dwgb", "dwgd", "dxb"))
    dwg2 = [dwg, dwgb]
    dx2 = [dx, dxb]
    bufs = {k: [] for k in ("sync", "gpsimd", "tensor", "vector", "scalar")}

    def mkemit(eng):
        def op(meth, *a, inc=None, amt=1, **kw):
            if inc is not None:
                inc.n += amt

                def f(e, a=a, kw=kw, s=inc, amt=amt, meth=meth):
                    getattr(e, meth)(*a, **kw).then_inc(s.h, amt)
            else:
                def f(e, a=a, kw=kw, meth=meth):
                    getattr(e, meth)(*a, **kw)
            bufs[eng].append(f)
        return op

    T, V, S, SY, G = (mkemit(e) for e in ("tensor", "vector", "scalar", "sync", "gpsimd"))

    def wait(eng, sem, val=None):
        v = sem.n if val is None else val
        assert v <= sem.n, (eng, v, sem.n)
        bufs[eng].append(lambda e, s=sem.h, v=v: e.wait_ge(s, v))

    plast = {}

    def psum_reuse(eng, key):
        if key in plast:
            s, cnt = plast[key]
            wait(eng, s, cnt)

    def psum_consumed(key, sem):
        plast[key] = (sem, sem.n)

    def dbg_dump(nm, src_ap):
        if nm in dbg_d:
            wait("sync", svv, svv.n)
            wait("sync", sss, sss.n)
            wait("sync", smm, smm.n)
            SY("dma_start", out=dbg_d[nm], in_=src_ap, inc=sst, amt=16)
            for eng in ("vector", "scalar", "tensor", "gpsimd"):
                wait(eng, sst, sst.n)

    SY("dma_start", out=identt[:], in_=d["ident"], inc=dx, amt=16)
    SY("dma_start", out=maskt[:], in_=d["maskv"], inc=dx, amt=16)
    SY("dma_start", out=onescb[:], in_=d["ones_col_bf"], inc=dx, amt=16)
    SY("dma_start", out=onesr[:], in_=d["ones_row"], inc=dx, amt=16)
    SY("dma_start", out=onesrb[:], in_=d["ones_row_bf"], inc=dx, amt=16)
    SY("dma_start", out=epsc[:], in_=d["eps_col"], inc=dx, amt=16)
    SY("dma_start", out=nhc[:], in_=d["nh_col"], inc=dx, amt=16)
    SY("dma_start", out=c_sb[:], in_=d["c_fm"], inc=dx, amt=16)
    for nm, dst in (("conv_w", convw), ("conv_b", convb), ("dt_b", dtb),
                    ("A_log", Aneg), ("Dvec", Dv), ("gate_sc", gsc), ("up_sc", usc),
                    ("down_sc", dscgb)):
        SY("dma_start", out=dst[:], in_=d[nm], inc=dx, amt=16)
    xpw_st = pv(0, F32, NIT * 160)
    dtp_st = pv(24 * KB, F32, NIT * 128)
    SY("dma_start", out=xpw_st[:], in_=d["xpwT"], inc=dwg2[0], amt=16)
    SY("dma_start", out=dtp_st[:], in_=d["dtpT"], inc=dwg2[1], amt=16)
    wait("scalar", dwg2[0], dwg2[0].n)
    wait("scalar", dwg2[1], dwg2[1].n)
    for c0 in range(0, NIT * 160, 1280):
        S("activation", xpw_sb[:, c0:c0 + 1280], xpw_st[:, c0:c0 + 1280], ACT.Identity)
    for c0 in range(0, NIT * 128, 2048):
        S("activation", dtp_sb[:, c0:c0 + 2048], dtp_st[:, c0:c0 + 2048], ACT.Identity)
    S("drain")
    S("nop", inc=sss)
    XPW_SS = sss.n
    CONST_DX, CONST_DWG = dx.n, (dwg2[0].n, dwg2[1].n)

    wait("scalar", dx, CONST_DX)
    S("activation", Aneg[:], Aneg[:], ACT.Exp)
    S("activation", c_bf[:], c_sb[:], ACT.Silu, inc=sss)
    wait("vector", sss, sss.n)
    V("tensor_scalar", Aneg[:], Aneg[:], -1.0, None, ALU.mult)
    V("memset", mod[:], 0.0, inc=svv)

    WMC = {}
    WMM = {}
    for hs in range(2 * NHT):
        ht, hf = hs // 2, hs % 2
        kb = hs % 2
        if hs >= 2:
            wait("sync", sss, WMC[hs - 2])
        else:
            wait("sync", sss, XPW_SS)
        SY("dma_start", out=wmst[kb][:], in_=d["wmod"][ht][:, hf * 6144:(hf + 1) * 6144],
           inc=dwg2[kb], amt=16)
        wait("scalar", dwg2[kb], dwg2[kb].n)
        if hs == 0:
            wait("scalar", sss, XPW_SS)
        if hs >= 2:
            wait("scalar", smm, WMM[hs - 2])
        for c0 in range(0, 6144, 2048):
            S("activation", wmsl[kb][:, c0:c0 + 2048], wmst[kb][:, c0:c0 + 2048],
              ACT.Identity, inc=sss if c0 == 4096 else None)
        WMC[hs] = sss.n
        wait("tensor", sss, sss.n)
        wait("tensor", svv, svv.n)
        for mt in range(48):
            T("matmul", pmod[:, hf * 48 + mt: hf * 48 + mt + 1],
              wmsl[kb][:, mt * 128:(mt + 1) * 128],
              c_bf[:, ht:ht + 1], start=True, stop=True, inc=smm if mt == 47 else None)
        WMM[hs] = smm.n
        if hf == 1:
            wait("vector", smm, smm.n)
            V("tensor_tensor", mod[:], mod[:], pmod[:, 0:96], ALU.add, inc=svv)
    psum_consumed("pmm0", svv)
    dbg_dump("modraw", mod[:])

    wait("vector", svv, svv.n)
    V("tensor_scalar", mod[:, CA1:CA1 + 16], mod[:, CA1:CA1 + 16], 1.0, None, ALU.add)
    V("tensor_scalar", mod[:, CB1:CB1 + 16], mod[:, CB1:CB1 + 16], 1.0, None, ALU.add)
    for c0 in (GA, GB):
        V("tensor_scalar", mtmp[:, 0:16], mod[:, c0:c0 + 16], 0.0, None, ALU.is_ge)
        V("drain")
        V("tensor_scalar", mtmp[:, 16:32], mtmp[:, 0:16], 2.0, -1.0, ALU.mult, ALU.add)
        V("tensor_scalar", mtmp[:, 32:48], mod[:, c0:c0 + 16], -1.0, None, ALU.mult)
        V("drain")
        V("tensor_tensor", mtmp[:, 48:64], mod[:, c0:c0 + 16], mtmp[:, 32:48], ALU.max)
        V("drain")
        V("tensor_scalar", mtmp[:, 48:64], mtmp[:, 48:64], 0.25, None, ALU.max)
        V("drain")
        V("tensor_tensor", mod[:, c0:c0 + 16], mtmp[:, 16:32], mtmp[:, 48:64], ALU.mult)
        V("drain")
    V("tensor_tensor", dscgb[:], dscgb[:], mod[:, GB:GB + 16], ALU.mult, inc=svv)
    MOD_V = svv.n
    dbg_dump("mod", mod[:])

    LNS = {}
    LNQ = [0]

    def ln_tile(xd, lt, h1v, stride):
        q = LNQ[0] & 1
        LNQ[0] += 1
        xr = xrow if q == 0 else xrow2
        if q in LNS:
            wait("sync", smm, LNS[q])
        SY("dma_start", out=xr[:], in_=xd[lt * 128:(lt + 1) * 128, :], inc=dx2[q], amt=16)
        wait("vector", dx2[q], dx2[q].n)
        V("tensor_reduce", strow2[:, 0:1], xr[:], AXX, ALU.add)
        V("drain")
        V("tensor_scalar", strow2[:, 1:2], strow2[:, 0:1], 1.0 / 2048.0, None, ALU.mult, inc=svv)
        wait("scalar", svv, svv.n)
        S("activation", outsl[:], xr[:], ACT.Square, inc=sss)
        wait("vector", sss, sss.n)
        V("tensor_reduce", strow2[:, 2:3], outsl[:], AXX, ALU.add)
        V("drain")
        V("tensor_scalar", strow2[:, 3:4], strow2[:, 2:3], 1.0 / 2048.0, None, ALU.mult)
        V("tensor_tensor", strow2[:, 4:5], strow2[:, 1:2], strow2[:, 1:2], ALU.mult)
        V("drain")
        V("tensor_tensor", strow2[:, 5:6], strow2[:, 3:4], strow2[:, 4:5], ALU.subtract, inc=svv)
        wait("scalar", svv, svv.n)
        S("activation", strow2[:, 6:7], strow2[:, 5:6], ACT.Ln, bias=epsc[:])
        S("drain")
        S("activation", strow2[:, 7:8], strow2[:, 6:7], ACT.Exp, scale=nhc[:], inc=sss)
        wait("vector", sss, sss.n)
        V("tensor_tensor", strow2[:, 8:9], strow2[:, 1:2], strow2[:, 7:8], ALU.mult)
        V("drain")
        V("tensor_scalar", strow2[:, 9:10], strow2[:, 8:9], -1.0, None, ALU.mult)
        V("drain")
        V("tensor_scalar", xr[:], xr[:], strow2[:, 7:8], strow2[:, 9:10],
          ALU.mult, ALU.add, inc=svv)
        wait("tensor", svv, svv.n)
        nl = stride // 128
        for ht in range(NHT):
            pk = (ht // 4) % 2
            if ht % 4 == 0:
                psum_reuse("tensor", f"ptr{pk}")
            T("transpose", ptr[pk][:, (ht % 4) * 128:(ht % 4 + 1) * 128],
              xr[:, ht * 128:(ht + 1) * 128], identt[:],
              inc=smm if ht % 4 == 3 else None)
            if ht % 4 == 3:
                wait("vector", smm, smm.n)
                for hh in range(ht - 3, ht + 1):
                    col = hh * stride + (lt % nl) * 128
                    V("tensor_scalar", h1v[:, col:col + 128],
                      ptr[pk][:, (hh % 4) * 128:(hh % 4 + 1) * 128],
                      mod[:, CA1 + hh:CA1 + hh + 1], mod[:, SA + hh:SA + hh + 1],
                      ALU.mult, ALU.add, inc=svv if hh == ht else None)
                psum_consumed(f"ptr{pk}", svv)
        LNS[q] = smm.n

    IP = {"k": 0}

    def inproj_slab(pt, stg, slb):
        k = IP["k"]
        st_, wb = stg[k % 2], slb[k % 2]
        if k >= 2:
            wait("sync", sss, IP[(k - 2, "c")])
        SY("dma_start", out=st_[:], in_=d["w_inT"][pt], inc=dwg2[k % 2], amt=16)
        wait("scalar", dwg2[k % 2], dwg2[k % 2].n)
        if k >= 2:
            wait("scalar", smm, IP[k - 2])
        S("activation", wb[:], st_[:], ACT.Identity, inc=sss)
        IP[(k, "c")] = sss.n
        IP["k"] = k + 1
        return k, wb

    PCV = [0]
    wait("sync", smm, WMM[2 * NHT - 1])
    for lh in range(2):
        for lt4 in range(4):
            if lh == 0 and lt4 == 0:
                wait("vector", svv, MOD_V)
            ln_tile(d["x_pre"], lh * 4 + lt4, h1_pre, 512)
        H1PRE_V = svv.n
        if lh == 1:
            dbg_dump("h1p", h1_pre[:])
        if lh == 0:
            wait("sync", sss, WMC[2 * NHT - 1])
        for pt in range(32):
            it = pt
            k, wb = inproj_slab(pt, wslp_st, wslp)
            wait("tensor", sss, sss.n)
            if pt == 0:
                wait("tensor", svv, H1PRE_V)
            pb = pmm[2 + k % 2]
            psum_reuse("tensor", f"pmm{2 + k % 2}")
            for ht in range(NHT):
                T("matmul", pb[:], wb[:, ht * 128:(ht + 1) * 128],
                  h1_pre[:, ht * 512: ht * 512 + 512],
                  start=(ht == 0), stop=(ht == NHT - 1),
                  inc=smm if ht == NHT - 1 else None)
            IP[k] = smm.n
            wait("scalar", smm, smm.n)
            if PCV[0]:
                wait("scalar", svv, PCV[0])
            S("activation", hin_t[:, 4:516], pb[:], ACT.Identity, inc=sss)
            psum_consumed(f"pmm{2 + k % 2}", sss)
            wait("vector", sss, sss.n)
            if lh == 0:
                V("memset", hin_t[:, 0:4], 0.0)
                V("tensor_copy", halo2[:, it * 4:(it + 1) * 4], hin_t[:, 512:516])
            else:
                V("tensor_copy", hin_t[:, 0:4], halo2[:, it * 4:(it + 1) * 4])
                V("tensor_copy", halo[:, it * 4:(it + 1) * 4], hin_t[:, 512:516])
            V("drain")
            V("tensor_scalar", cacc[:, 0:512], hin_t[:, 4:516],
              convw[:, it * 4 + 3:it * 4 + 4], convb[:, it:it + 1], ALU.mult, ALU.add)
            for kk in (2, 1, 0):
                V("drain")
                V("scalar_tensor_tensor", cacc[:, 0:512], hin_t[:, 1 + kk:513 + kk],
                  convw[:, it * 4 + kk:it * 4 + kk + 1], cacc[:, 0:512], ALU.mult, ALU.add)
            V("drain")
            V("nop", inc=svv)
            PCV[0] = svv.n
            wait("scalar", svv, svv.n)
            S("activation", u_pre[:, it * TOK + lh * 512: it * TOK + lh * 512 + 512],
              cacc[:, 0:512], ACT.Silu, inc=sss)
    dbg_dump("u_pre", u_pre[:])

    wait("sync", smm, smm.n)
    SY("dma_start", out=sel_sb[:], in_=d["sel"], inc=dx, amt=16)

    def xproj(uv, off):
        wait("tensor", sss, sss.n)
        wait("tensor", dwg2[0], CONST_DWG[0])
        wait("tensor", dwg2[1], CONST_DWG[1])
        XPT = [0]
        for lt in range(NLT):
            pb = pmm[lt % 2]
            psum_reuse("tensor", f"pmm{lt % 2}")
            for it in range(NIT):
                T("matmul", pb[:, 0:160],
                  uv[:, it * TOK + lt * 128: it * TOK + lt * 128 + 128],
                  xpw_sb[:, it * 160:(it + 1) * 160],
                  start=(it == 0), stop=(it == NIT - 1),
                  inc=smm if it == NIT - 1 else None)
            wait("scalar", smm, smm.n)
            if XPT[0]:
                wait("scalar", smm, XPT[0])
            S("activation", ssm_sb[:], pb[:, 0:160], ACT.Identity, inc=sss)
            psum_consumed(f"pmm{lt % 2}", sss)
            wait("tensor", sss, sss.n)
            pk = lt % 2
            psum_reuse("tensor", f"ptr{pk}")
            T("transpose", ptr[pk][:, 0:128], ssm_sb[:, 0:128], identt[:])
            T("transpose", ptr[pk][0:32, 128:256], ssm_sb[:, 128:160], identt[:], inc=smm)
            XPT[0] = smm.n
            if off == TOK and lt == 3:
                dbg_dump("ssm3", ssm_sb[:])
            wait("vector", smm, smm.n)
            V("tensor_copy", ts_T[:, off + lt * 128: off + lt * 128 + 128], ptr[pk][:, 0:128])
            V("tensor_copy", bc_T[0:32, off + lt * 128: off + lt * 128 + 128],
              ptr[pk][0:32, 128:256], inc=svv)
            psum_consumed(f"ptr{pk}", svv)

    xproj(u_pre, 0)
    dbg_dump("ts_pre", ts_T[:, 0:TOK])
    dbg_dump("bc_pre", bc_T[:, 0:TOK])

    def scan_phase(uv, off, own):
        DAF = {}
        NCH = {}
        PW2 = [0]
        if own:
            wait("vector", svv, svv.n)
            V("tensor_scalar", carry[:], carry[:], maskt[:], None, ALU.mult)
            V("drain")
            V("nop", inc=svv)
        SCAN_V0 = svv.n
        first = True
        for g in range(NIT // 2):
            for j in range(2):
                it = g * 2 + j
                for lhh in range(2):
                    psum_reuse("tensor", f"pmm{2 + lhh}")
                    if first:
                        wait("tensor", svv, svv.n)
                        wait("tensor", dwg2[0], CONST_DWG[0])
                        wait("tensor", dwg2[1], CONST_DWG[1])
                        wait("tensor", dx, dx.n)
                        first = False
                    T("matmul", pmm[2 + lhh][:], dtp_sb[:, it * 128:(it + 1) * 128],
                      ts_T[:, off + lhh * 512: off + lhh * 512 + 512],
                      start=True, stop=True, inc=smm if lhh == 1 else None)
                wait("scalar", smm, smm.n)
                if PW2[0]:
                    wait("scalar", svv, PW2[0])
                dtv = dt4[:, j * TOK:(j + 1) * TOK]
                S("drain")
                S("activation", dtv[:, 0:512], pmm[2][:], ACT.Exp, bias=dtb[:, it:it + 1])
                S("activation", dtv[:, 512:1024], pmm[3][:], ACT.Exp, bias=dtb[:, it:it + 1],
                  inc=sss)
                psum_consumed("pmm2", sss)
                psum_consumed("pmm3", sss)
                S("drain")
                S("activation", dtv[:], dtv[:], ACT.Ln, bias=1.0, inc=sss)
                wait("vector", sss, sss.n)
                V("tensor_tensor", w4[:, j * TOK:(j + 1) * TOK], dtv[:],
                  uv[:, it * TOK:(it + 1) * TOK], ALU.mult)
                V("drain")
                V("nop", inc=svv)
            W2_V = svv.n
            PW2[0] = W2_V
            if not own and g == 0:
                dbg_dump("dt4g0", dt4[:])
                dbg_dump("w4g0", w4[:])
            if own and g == 0:
                dbg_dump("o_dt4", dt4[:])
                dbg_dump("o_w4", w4[:])
                dbg_dump("o_ts", ts_T[:])
                dbg_dump("o_bcT", bc_T[:])
                dbg_dump("o_u0", uv[:, 0:2 * TOK])
            for n in range(NST):
                ns = g * NST + n
                q = ns % 2
                Bq, Cq = Bbv[q], Cbv[q]
                psum_reuse("tensor", "ptr0")
                psum_reuse("tensor", "ptr1")
                T("matmul", ptr[0][:], sel_sb[:, n * 128:(n + 1) * 128],
                  bc_T[0:32, off:off + 512], start=True, stop=True)
                T("matmul", ptr[1][:], sel_sb[:, n * 128:(n + 1) * 128],
                  bc_T[0:32, off + 512:off + 1024], start=True, stop=True, inc=smm)
                wait("scalar", smm, smm.n)
                wait("scalar", svv, NCH.get(ns - 2, SCAN_V0))
                S("activation", Bq[:, 0:512], ptr[0][:], ACT.Identity)
                S("activation", Bq[:, 512:1024], ptr[1][:], ACT.Identity, inc=sss)
                psum_consumed("ptr0", sss)
                psum_consumed("ptr1", sss)
                if own:
                    psum_reuse("tensor", "ptr0")
                    psum_reuse("tensor", "ptr1")
                    T("matmul", ptr[0][:], sel_sb[:, (16 + n) * 128:(17 + n) * 128],
                      bc_T[0:32, off:off + 512], start=True, stop=True)
                    T("matmul", ptr[1][:], sel_sb[:, (16 + n) * 128:(17 + n) * 128],
                      bc_T[0:32, off + 512:off + 1024], start=True, stop=True, inc=smm)
                    wait("scalar", smm, smm.n)
                    S("activation", Cq[:, 0:512], ptr[0][:], ACT.Identity)
                    S("activation", Cq[:, 512:1024], ptr[1][:], ACT.Identity, inc=sss)
                    psum_consumed("ptr0", sss)
                    psum_consumed("ptr1", sss)
                for j in range(2):
                    it = g * 2 + j
                    nb = (n * 2 + j) % 2
                    if nb in DAF:
                        wait("scalar", svv, DAF[nb])
                    S("activation", dA_t[nb][:], dt4[:, j * TOK:(j + 1) * TOK], ACT.Exp,
                      scale=Aneg[:, it * NST + n: it * NST + n + 1], inc=sss)
                    wait("vector", sss, sss.n)
                    if n == 0 and j == 0:
                        wait("vector", svv, W2_V)
                    V("tensor_tensor", dBx_t[:], w4[:, j * TOK:(j + 1) * TOK], Bq[:], ALU.mult)
                    V("drain")
                    if own:
                        V("tensor_tensor_scan", h_t[:], dA_t[nb][:], dBx_t[:],
                          carry[:, it * NST + n: it * NST + n + 1], ALU.mult, ALU.add)
                        V("drain")
                        if n == 0:
                            V("tensor_tensor", y4[:, j * TOK:(j + 1) * TOK],
                              Cq[:], h_t[:], ALU.mult, inc=svv)
                        else:
                            V("tensor_tensor", yy_t[:], Cq[:], h_t[:], ALU.mult)
                            V("drain")
                            V("tensor_tensor", y4[:, j * TOK:(j + 1) * TOK],
                              y4[:, j * TOK:(j + 1) * TOK], yy_t[:], ALU.add, inc=svv)
                    else:
                        V("tensor_tensor_scan", h_t[:], dA_t[nb][:], dBx_t[:], 0.0,
                          ALU.mult, ALU.add)
                        V("drain")
                        V("tensor_copy", carry[:, it * NST + n: it * NST + n + 1],
                          h_t[:, TOK - 1:TOK], inc=svv)
                    if own and g == 0 and n == 0 and j == 0:
                        dbg_dump("o_Bb", Bq[:])
                        dbg_dump("o_dA", dA_t[nb][:])
                        dbg_dump("o_dBx", dBx_t[:])
                        dbg_dump("o_h", h_t[:])
                        dbg_dump("o_Cb", Cq[:])
                        dbg_dump("o_y4", y4[:])
                    if own and g == 0 and n == 1 and j == 0:
                        dbg_dump("o_y4n1", y4[:])
                    if not own and g == 0 and n == 0 and j == 0:
                        dbg_dump("dA00", dA_t[nb][:])
                        dbg_dump("dBx00", dBx_t[:])
                        dbg_dump("h00", h_t[:])
                        dbg_dump("Bb0", Bq[:])
                    DAF[nb] = svv.n
                NCH[ns] = svv.n
            if own:
                for j in range(2):
                    it = g * 2 + j
                    V("drain")
                    V("scalar_tensor_tensor", uv[:, it * TOK:(it + 1) * TOK],
                      uv[:, it * TOK:(it + 1) * TOK], Dv[:, it:it + 1],
                      y4[:, j * TOK:(j + 1) * TOK], ALU.mult, ALU.add, inc=svv)
                if g == 0:
                    dbg_dump("o_fin0", uv[:, 0:TOK])
                    dbg_dump("o_y4f", y4[:])

    scan_phase(u_pre, 0, own=False)
    dbg_dump("carry", carry[:])

    wait("sync", svv, svv.n)
    for lt in range(NLT):
        ln_tile(d["x_own"], lt, h1_own, TOK)
    H1OWN_V = svv.n
    dbg_dump("h1o", h1_own[:])
    wait("sync", svv, svv.n)
    wait("scalar", svv, svv.n)
    for pt in range(32):
        it = pt
        k, wb = inproj_slab(pt, wslo_st, wslo)
        wait("tensor", sss, sss.n)
        if pt == 0:
            wait("tensor", svv, H1OWN_V)
        for lhh in range(2):
            pb = pmm[(k % 2) * 2 + lhh]
            psum_reuse("tensor", f"pmm{(k % 2) * 2 + lhh}")
            for ht in range(NHT):
                T("matmul", pb[:], wb[:, ht * 128:(ht + 1) * 128],
                  h1_own[:, ht * TOK + lhh * 512: ht * TOK + lhh * 512 + 512],
                  start=(ht == 0), stop=(ht == NHT - 1),
                  inc=smm if ht == NHT - 1 else None)
        IP[k] = smm.n
        wait("scalar", smm, smm.n)
        if PCV[0]:
            wait("scalar", svv, PCV[0])
        S("activation", hin_t[:, 4:516], pmm[(k % 2) * 2][:], ACT.Identity)
        S("activation", hin_t[:, 516:1028], pmm[(k % 2) * 2 + 1][:], ACT.Identity, inc=sss)
        psum_consumed(f"pmm{(k % 2) * 2}", sss)
        psum_consumed(f"pmm{(k % 2) * 2 + 1}", sss)
        wait("vector", sss, sss.n)
        V("tensor_scalar", hin_t[:, 0:4], halo[:, it * 4:(it + 1) * 4],
          maskt[:], None, ALU.mult)
        V("drain")
        V("tensor_scalar", cacc[:], hin_t[:, 4:4 + TOK],
          convw[:, it * 4 + 3:it * 4 + 4], convb[:, it:it + 1], ALU.mult, ALU.add)
        for kk in (2, 1, 0):
            V("drain")
            V("scalar_tensor_tensor", cacc[:], hin_t[:, 1 + kk:1 + kk + TOK],
              convw[:, it * 4 + kk:it * 4 + kk + 1], cacc[:], ALU.mult, ALU.add)
        V("drain")
        V("nop", inc=svv)
        PCV[0] = svv.n
        wait("scalar", svv, svv.n)
        S("activation", u_y[:, it * TOK:(it + 1) * TOK], cacc[:], ACT.Silu, inc=sss)
    dbg_dump("u_own", u_y[:])

    dbg_dump("xpw", xpw_sb[:])
    dbg_dump("dtp", dtp_sb[:])
    wait("sync", sss, IP[(95, "c")])
    SY("dma_start", out=sel_sb[:], in_=d["sel"], inc=dx, amt=16)
    xproj(u_y, TOK)
    dbg_dump("ts_own", ts_T[:, TOK:2 * TOK])
    dbg_dump("bc_own", bc_T[:, TOK:2 * TOK])
    scan_phase(u_y, TOK, own=True)
    dbg_dump("yraw", u_y[:])

    wait("sync", svv, svv.n)
    for pt in range(32, 64):
        it = pt - 32
        k, wb = inproj_slab(pt, wslo_st, wslo)
        wait("tensor", sss, sss.n)
        for lhh in range(2):
            pb = pmm[(k % 2) * 2 + lhh]
            psum_reuse("tensor", f"pmm{(k % 2) * 2 + lhh}")
            for ht in range(NHT):
                T("matmul", pb[:], wb[:, ht * 128:(ht + 1) * 128],
                  h1_own[:, ht * TOK + lhh * 512: ht * TOK + lhh * 512 + 512],
                  start=(ht == 0), stop=(ht == NHT - 1),
                  inc=smm if ht == NHT - 1 else None)
        IP[k] = smm.n
        wait("scalar", smm, smm.n)
        wait("scalar", svv, svv.n)
        S("activation", sgt[:, 0:512], pmm[(k % 2) * 2][:], ACT.Silu)
        S("activation", sgt[:, 512:1024], pmm[(k % 2) * 2 + 1][:], ACT.Silu, inc=sss)
        psum_consumed(f"pmm{(k % 2) * 2}", sss)
        psum_consumed(f"pmm{(k % 2) * 2 + 1}", sss)
        wait("vector", sss, sss.n)
        V("tensor_tensor", u_y[:, it * TOK:(it + 1) * TOK],
          u_y[:, it * TOK:(it + 1) * TOK], sgt[:], ALU.mult, inc=svv)
    YG_V = svv.n
    dbg_dump("ygated", u_y[:])

    WO = {}
    WOC = {}
    wait("sync", svv, svv.n)
    for ht in range(NHT):
        kb2 = ht % 2
        if ht >= 1:
            wait("sync", sss, WOC[ht - 1])
        SY("dma_start", out=wo_st[:], in_=d["woutT"][ht], inc=dwg2[0], amt=16)
        wait("scalar", dwg2[0], dwg2[0].n)
        if ht >= 2:
            wait("scalar", smm, WO[ht - 2])
        S("activation", wosl[kb2][:, 0:2048], wo_st[:, 0:2048], ACT.Identity)
        S("activation", wosl[kb2][:, 2048:4096], wo_st[:, 2048:4096], ACT.Identity, inc=sss)
        WOC[ht] = sss.n
        wait("tensor", sss, sss.n)
        if ht == 0:
            wait("tensor", svv, YG_V)
        for lhh in range(2):
            pb = pmm[kb2 * 2 + lhh]
            psum_reuse("tensor", f"pmm{kb2 * 2 + lhh}")
            for it in range(NIT):
                T("matmul", pb[:], wosl[kb2][:, it * 128:(it + 1) * 128],
                  u_y[:, it * TOK + lhh * 512: it * TOK + lhh * 512 + 512],
                  start=(it == 0), stop=(it == NIT - 1),
                  inc=smm if it == NIT - 1 else None)
        WO[ht] = smm.n
        wait("vector", smm, smm.n)
        for lhh in range(2):
            V("tensor_scalar", x2[:, ht * TOK + lhh * 512: ht * TOK + lhh * 512 + 512],
              pmm[kb2 * 2 + lhh][:], mod[:, GA + ht:GA + ht + 1], None, ALU.mult,
              inc=svv if lhh == 1 else None)
        psum_consumed(f"pmm{kb2 * 2}", svv)
        psum_consumed(f"pmm{kb2 * 2 + 1}", svv)

    for lt in range(NLT):
        wait("sync", smm, smm.n)
        SY("dma_start", out=xrow[:], in_=d["x_own"][lt * 128:(lt + 1) * 128, :], inc=dx, amt=16)
        wait("tensor", dx, dx.n)
        if lt == 0:
            wait("tensor", svv, svv.n)
        for ht in range(NHT):
            pk = (ht // 4) % 2
            if ht % 4 == 0:
                psum_reuse("tensor", f"ptr{pk}")
            T("transpose", ptr[pk][:, (ht % 4) * 128:(ht % 4 + 1) * 128],
              xrow[:, ht * 128:(ht + 1) * 128], identt[:],
              inc=smm if ht % 4 == 3 else None)
            if ht % 4 == 3:
                wait("vector", smm, smm.n)
                for hh in range(ht - 3, ht + 1):
                    col = hh * TOK + lt * 128
                    V("tensor_tensor", x2[:, col:col + 128], x2[:, col:col + 128],
                      ptr[pk][:, (hh % 4) * 128:(hh % 4 + 1) * 128], ALU.add,
                      inc=svv if hh == ht else None)
                psum_consumed(f"ptr{pk}", svv)
    X2_V = svv.n
    dbg_dump("x2", x2[:])

    for lhh in range(2):
        psum_reuse("tensor", "pst0")
        psum_reuse("tensor", "pst1")
        wait("tensor", svv, X2_V)
        wait("scalar", svv, X2_V)
        XQT = [0]
        for ht in range(NHT):
            col = ht * TOK + lhh * 512
            if XQT[0]:
                wait("scalar", smm, XQT[0])
            S("activation", xsq[:], x2[:, col:col + 512], ACT.Square, inc=sss)
            wait("tensor", sss, sss.n)
            T("matmul", pst[0][:], onescb[:], x2[:, col:col + 512],
              start=(ht == 0), stop=(ht == NHT - 1))
            T("matmul", pst[1][:], onescb[:], xsq[:],
              start=(ht == 0), stop=(ht == NHT - 1), inc=smm)
            XQT[0] = smm.n
        wait("vector", smm, smm.n)
        V("tensor_scalar", srow_a[0:1, 0:512], pst[0][0:1, :], 1.0 / 2048.0, None, ALU.mult)
        V("tensor_scalar", srow_a[0:1, 512:1024], pst[1][0:1, :], 1.0 / 2048.0, None, ALU.mult)
        V("drain")
        V("tensor_tensor", srow_b[0:1, 0:512], srow_a[0:1, 0:512], srow_a[0:1, 0:512],
          ALU.mult)
        V("drain")
        V("tensor_tensor", srow_b[0:1, 512:1024], srow_a[0:1, 512:1024], srow_b[0:1, 0:512],
          ALU.subtract, inc=svv)
        psum_consumed("pst0", svv)
        psum_consumed("pst1", svv)
        wait("scalar", svv, svv.n)
        S("activation", srow_b[0:1, 0:512], srow_b[0:1, 512:1024], ACT.Ln, bias=epsc[0:1, :])
        S("drain")
        S("activation", srow_a[0:1, 512:1024], srow_b[0:1, 0:512], ACT.Exp, scale=nhc[0:1, :],
          inc=sss)
        wait("vector", sss, sss.n)
        V("tensor_tensor", srow_b[0:1, 512:1024], srow_a[0:1, 0:512], srow_a[0:1, 512:1024],
          ALU.mult)
        V("drain")
        V("tensor_scalar", srow_b[0:1, 512:1024], srow_b[0:1, 512:1024], -1.0, None, ALU.mult,
          inc=svv)
        wait("tensor", svv, svv.n)
        psum_reuse("tensor", "ptr0")
        psum_reuse("tensor", "ptr1")
        T("matmul", ptr[0][:], onesr[:], srow_a[0:1, 512:1024], start=True, stop=True)
        T("matmul", ptr[1][:], onesr[:], srow_b[0:1, 512:1024], start=True, stop=True, inc=smm)
        wait("vector", smm, smm.n)
        for ht in range(NHT):
            col = ht * TOK + lhh * 512
            V("tensor_tensor", h2[:, col:col + 512], x2[:, col:col + 512], ptr[0][:], ALU.mult)
            V("drain")
            V("tensor_tensor", h2[:, col:col + 512], h2[:, col:col + 512], ptr[1][:], ALU.add)
            V("drain")
            V("tensor_scalar", h2[:, col:col + 512], h2[:, col:col + 512],
              mod[:, CB1 + ht:CB1 + ht + 1], mod[:, SB_ + ht:SB_ + ht + 1],
              ALU.mult, ALU.add, inc=svv if ht == NHT - 1 else None)
        psum_consumed("ptr0", svv)
        psum_consumed("ptr1", svv)
    H2_V = svv.n
    dbg_dump("h2", h2[:])

    GUF = {}
    DWF = {}
    wait("sync", smm, smm.n)
    wait("sync", svv, svv.n)
    OUT_N = [0]
    NG = 6
    for lhh in range(2):
        for grp in range(NG):
            for itg in range(8):
                it = grp * 8 + itg
                kb2 = itg % 2
                if ("gust",) in GUF:
                    wait("sync", svv, GUF[("gust",)])
                SY("dma_start", out=gu_st[:], in_=d["gate_w"][it], inc=dwg2[0], amt=16)
                wait("vector", dwg2[0], dwg2[0].n)
                if (kb2, "g") in GUF:
                    wait("vector", smm, GUF[(kb2, "g")])
                V("tensor_copy", g8sl[kb2].ap().bitcast(FP8), gu_st[:], inc=svv)
                wait("sync", svv, svv.n)
                SY("dma_start", out=gu_st[:], in_=d["up_w"][it], inc=dwg2[0], amt=16)
                wait("vector", dwg2[0], dwg2[0].n)
                if (kb2, "u") in GUF:
                    wait("vector", smm, GUF[(kb2, "u")])
                V("tensor_copy", u8sl[kb2].ap().bitcast(FP8), gu_st[:], inc=svv)
                GUF[("gust",)] = svv.n
                wait("tensor", svv, svv.n)
                if (lhh, grp, itg) == (0, 0, 0):
                    wait("tensor", svv, H2_V)
                pg, pu = pmm[kb2 * 2], pmm[kb2 * 2 + 1]
                psum_reuse("tensor", f"pmm{kb2 * 2}")
                psum_reuse("tensor", f"pmm{kb2 * 2 + 1}")
                for ht in range(NHT):
                    col = ht * TOK + lhh * 512
                    T("matmul", pg[:], g8sl[kb2].ap().bitcast(FP8)[:, ht * 128:(ht + 1) * 128],
                      h2[:, col:col + 512], start=(ht == 0), stop=(ht == NHT - 1))
                    T("matmul", pu[:], u8sl[kb2].ap().bitcast(FP8)[:, ht * 128:(ht + 1) * 128],
                      h2[:, col:col + 512], start=(ht == 0), stop=(ht == NHT - 1),
                      inc=smm if ht == NHT - 1 else None)
                GUF[(kb2, "g")] = smm.n
                GUF[(kb2, "u")] = smm.n
                wait("scalar", smm, smm.n)
                wait("scalar", svv, svv.n)
                S("activation", sgt[:, 0:512], pg[:], ACT.Silu, scale=gsc[:, it:it + 1])
                S("activation", sut[:, 0:512], pu[:], ACT.Identity, scale=usc[:, it:it + 1],
                  inc=sss)
                psum_consumed(f"pmm{kb2 * 2}", sss)
                psum_consumed(f"pmm{kb2 * 2 + 1}", sss)
                wait("vector", sss, sss.n)
                V("tensor_tensor", fuse[:, itg * 512:(itg + 1) * 512],
                  sgt[:, 0:512], sut[:, 0:512], ALU.mult, inc=svv)
            FUSE_V = svv.n
            if lhh == 0 and grp == 0:
                dbg_dump("fuse0", fuse[:])
            for ht in range(NHT):
                kb2 = ht % 2
                if ("dnst",) in DWF:
                    wait("sync", svv, DWF[("dnst",)])
                SY("dma_start", out=dn_st[:],
                   in_=d["down_w"][ht][:, grp * 1024:(grp + 1) * 1024], inc=dwgd, amt=16)
                wait("vector", dwgd, dwgd.n)
                if (kb2, "d") in DWF:
                    wait("vector", smm, DWF[(kb2, "d")])
                V("tensor_copy", d8sl[kb2].ap().bitcast(FP8), dn_st[:], inc=svv)
                DWF[("dnst",)] = svv.n
                wait("tensor", svv, svv.n)
                if ht == 0:
                    wait("tensor", svv, FUSE_V)
                pb = pmm[kb2 * 2]
                psum_reuse("tensor", f"pmm{kb2 * 2}")
                for itg in range(8):
                    T("matmul", pb[:], d8sl[kb2].ap().bitcast(FP8)[:, itg * 128:(itg + 1) * 128],
                      fuse[:, itg * 512:(itg + 1) * 512],
                      start=(itg == 0), stop=(itg == 7),
                      inc=smm if itg == 7 else None)
                DWF[(kb2, "d")] = smm.n
                wait("vector", smm, smm.n)
                dcol = ht * 512
                if grp == 0:
                    V("tensor_copy", dacc[:, dcol:dcol + 512], pb[:], inc=svv)
                else:
                    V("tensor_tensor", dacc[:, dcol:dcol + 512], dacc[:, dcol:dcol + 512],
                      pb[:], ALU.add, inc=svv)
                psum_consumed(f"pmm{kb2 * 2}", svv)
        DACC_V = svv.n
        if lhh == 0:
            dbg_dump("dacc0", dacc[:])
        for lt in range(4):
            if OUT_N[0] > 0:
                wait("vector", sst, OUT_N[0])
            for ht in range(NHT):
                pk = ht % 2
                if lt == 0 and ht < 2:
                    wait("vector", svv, DACC_V)
                V("scalar_tensor_tensor", p9f[:],
                  dacc[:, ht * 512 + lt * 128: ht * 512 + lt * 128 + 128],
                  dscgb[:, ht:ht + 1],
                  x2[:, ht * TOK + lhh * 512 + lt * 128: ht * TOK + lhh * 512 + lt * 128 + 128],
                  ALU.mult, ALU.add, inc=svv)
                psum_reuse("tensor", f"ptr{pk}")
                wait("tensor", svv, svv.n)
                T("transpose", ptr[pk][:, 0:128], p9f[:], identt[:], inc=smm)
                wait("vector", smm, smm.n)
                V("tensor_copy", outsl[:, ht * 128:(ht + 1) * 128], ptr[pk][:, 0:128],
                  inc=svv)
                psum_consumed(f"ptr{pk}", svv)
            wait("sync", svv, svv.n)
            row0 = lhh * 512 + lt * 128
            SY("dma_start", out=out_d[row0:row0 + 128, :], in_=outsl[:], inc=sst, amt=16)
            OUT_N[0] = sst.n
    wait("sync", sst, sst.n)

    with nc.Block() as block:
        @block.sync
        def _(e):
            for fn in bufs["sync"]:
                fn(e)

        @block.gpsimd
        def _(e):
            for fn in bufs["gpsimd"]:
                fn(e)

        @block.tensor
        def _(e):
            for fn in bufs["tensor"]:
                fn(e)

        @block.vector
        def _(e):
            for fn in bufs["vector"]:
                fn(e)

        @block.scalar
        def _(e):
            for fn in bufs["scalar"]:
                fn(e)

    ctx.close()
    return nc, bufs


def _emit(nc, bufs):
    with nc.Block() as block:
        @block.sync
        def _(e):
            for f in bufs["sync"]:
                f(e)

        @block.gpsimd
        def _(e):
            for f in bufs["gpsimd"]:
                f(e)

        @block.tensor
        def _(e):
            for f in bufs["tensor"]:
                f(e)

        @block.vector
        def _(e):
            for f in bufs["vector"]:
                f(e)

        @block.scalar
        def _(e):
            for f in bufs["scalar"]:
                f(e)


def host_inputs(inputs):
    f = np.ascontiguousarray
    ipw = np.asarray(inputs["L0_attn_in_proj_w"])
    w_inT = f(ipw.T.reshape(16, 128, 64, 128).transpose(2, 1, 0, 3).reshape(64, 128, 2048))
    shared = dict(
        w_inT=w_inT,
        conv_w=f(np.asarray(inputs["L0_attn_conv_w"]).reshape(NIT, 128, 4)
                 .transpose(1, 0, 2).reshape(128, NIT * 4)),
        conv_b=f(np.asarray(inputs["L0_attn_conv_b"]).reshape(NIT, 128).T),
        xpwT=f(np.asarray(inputs["L0_attn_x_proj_w"]).T.reshape(NIT, 128, 160)
               .transpose(1, 0, 2).reshape(128, NIT * 160)),
        dtpT=f(np.asarray(inputs["L0_attn_dt_proj_w"]).T),
        dt_b=f(np.asarray(inputs["L0_attn_dt_proj_b"]).reshape(NIT, 128).T),
        A_log=f(np.asarray(inputs["L0_attn_A_log"]).reshape(NIT, 128, NST)
                .transpose(1, 0, 2).reshape(128, NIT * NST)),
        Dvec=f(np.asarray(inputs["L0_attn_D"]).reshape(NIT, 128).T),
        woutT=f(np.asarray(inputs["L0_attn_out_proj_w"]).T.reshape(NIT, 128, NHT, 128)
                .transpose(2, 1, 0, 3).reshape(NHT, 128, 4096)),
        gate_w=f(np.asarray(inputs["L0_ffn_gate_w"]).reshape(NHT, 128, NFT, 128)
                 .transpose(2, 1, 0, 3).reshape(NFT, 128, 2048)),
        up_w=f(np.asarray(inputs["L0_ffn_up_w"]).reshape(NHT, 128, NFT, 128)
               .transpose(2, 1, 0, 3).reshape(NFT, 128, 2048)),
        down_w=f(np.asarray(inputs["L0_ffn_down_w"]).reshape(NFT, 128, NHT, 128)
                 .transpose(2, 1, 0, 3).reshape(NHT, 128, 6144)),
        gate_sc=f(np.asarray(inputs["L0_ffn_gate_scale"]).reshape(NFT, 128).T),
        up_sc=f(np.asarray(inputs["L0_ffn_up_scale"]).reshape(NFT, 128).T),
        down_sc=f(np.asarray(inputs["L0_ffn_down_scale"]).reshape(NHT, 128).T),
        wmod=f(np.asarray(inputs["L0_wmod"]).reshape(NHT, 128, 12288)),
        ident=np.eye(128, dtype=np.float32),
        ones_col_bf=np.ones((128, 1), ml_dtypes.bfloat16),
        ones_row=np.ones((1, 128), np.float32),
        ones_row_bf=np.ones((1, 128), ml_dtypes.bfloat16),
        eps_col=np.full((128, 1), 1e-5, np.float32),
        nh_col=np.full((128, 1), -0.5, np.float32),
        sel=np.ascontiguousarray(
            np.tile(np.eye(32, dtype=ml_dtypes.bfloat16)[:, :, None], (1, 1, 128))
            .reshape(32, 32 * 128)),
    )
    x = np.asarray(inputs["L0_attn_x"])
    c = np.asarray(inputs["L0_c"])
    maps = []
    for core in range(8):
        b, half = core >> 1, core & 1
        m = dict(shared)
        m["x_own"] = f(x[b, half * TOK:(half + 1) * TOK])
        m["x_pre"] = f(x[b, (1 - half) * TOK:(2 - half) * TOK])
        m["maskv"] = np.full((128, 1), float(half), np.float32)
        m["c_fm"] = f(c[b].reshape(NHT, 128).T)
        maps.append(m)
    return maps


def kernel(**inputs):
    nc, bufs = build_program()
    maps = host_inputs(inputs)
    res = run_bass_kernel_spmd(nc, maps, core_ids=list(range(8)))
    out = np.empty((B, L, H), np.float32)
    for core in range(8):
        b, half = core >> 1, core & 1
        out[b, half * TOK:(half + 1) * TOK] = res.results[core]["out"]
    return out
